# version 05
# baseline (speedup 1.0000x reference)
import numpy as np
from contextlib import ExitStack

import concourse.bass as bass
import concourse.mybir as mybir
from concourse.bass_utils import run_bass_kernel_spmd

F32 = mybir.dt.float32
BF16 = mybir.dt.bfloat16
AF = mybir.ActivationFunctionType
ALU = mybir.AluOpType
AX = mybir.AxisListType

P = 128
D = 1024
NH = 16
HD = 64
DH = 4096
NE = 8
MH = 2048
EPS_LN = 1e-5
EPS_LA = 1e-6
L2_EPS2 = 1e-24
ATT_SCALE = 0.125


class Sem:
    def __init__(self, h):
        self.h = h
        self.n = 0


class Ctx:

    def __init__(self, nc, ex, tag, sem_ex=None):
        self.nc = nc
        self.ex = ex
        self.sem_ex = sem_ex if sem_ex is not None else ex
        self.tag = tag
        self.blk = ex.enter_context(nc.Block())
        self.dirty = {"v": False, "s": False, "g": False}
        self.pend = {k: [] for k in ["v", "s", "g", "t", "y"]}
        self.waited = {k: {} for k in ["v", "s", "g", "t", "y"]}
        self.sem = {}
        for name in ["ld", "pe", "vs", "ac", "gp", "stq"]:
            self.sem[name] = Sem(self.sem_ex.enter_context(nc.semaphore(f"{tag}_{name}")))

    def new_sem(self, name):
        return Sem(self.sem_ex.enter_context(self.nc.semaphore(f"{self.tag}_{name}")))

    def barrier_out(self, done):
        self.flush()
        for section, eng, drain in [(self.blk.vector, self.nc.vector, True),
                                    (self.blk.scalar, self.nc.scalar, True),
                                    (self.blk.gpsimd, self.nc.gpsimd, True),
                                    (self.blk.tensor, self.nc.tensor, False),
                                    (self.blk.sync, self.nc.sync, False)]:
            def f(_e, eng=eng, drain=drain):
                ins = eng.drain() if drain else eng.nop()
                ins.then_inc(done.h, 1)
                done.n += 1
            section(f)

    def barrier_in(self, done, val):
        self.flush()
        for section in [self.blk.vector, self.blk.scalar, self.blk.gpsimd,
                        self.blk.tensor, self.blk.sync]:
            def f(_e, val=val):
                _e.wait_ge(done.h, val)
            section(f)

    class _Pre:
        _raw = {"wait_ge", "dma_start", "nop", "drain", "indirect_dma_start"}

        def __init__(self, ctx, key, eng):
            object.__setattr__(self, "_c", (ctx, key, eng))

        def __getattr__(self, name):
            ctx, key, eng = object.__getattribute__(self, "_c")
            fn = getattr(eng, name)
            if name in Ctx._Pre._raw:
                return fn

            def wrap(*a, **k):
                if ctx.dirty[key]:
                    eng.drain()
                ctx.dirty[key] = True
                return fn(*a, **k)
            return wrap

    class _Rec:

        def __init__(self, eng):
            object.__setattr__(self, "_e", eng)
            object.__setattr__(self, "last", None)

        def __getattr__(self, name):
            eng = object.__getattribute__(self, "_e")
            fn = getattr(eng, name)
            if name in ("wait_ge",):
                return fn

            def wrap(*a, **k):
                r = fn(*a, **k)
                try:
                    object.__setattr__(self, "last", r)
                except Exception:
                    pass
                return r
            return wrap

    def _mk(self, section, eng, key, mile_sem):
        is_da = key in ("v", "s", "g")

        def run(body, waits=(), mile=True):
            seen = self.waited[key]
            todo = []
            for (sem, val) in waits:
                if val is None:
                    continue
                kid = id(sem)
                if seen.get(kid, -1) < val:
                    seen[kid] = val
                    todo.append((sem.h, val))
            mval = None
            if mile:
                mile_sem.n += 1
                mval = mile_sem.n

            def emit():
                _e = eng
                for (h, val) in todo:
                    _e.wait_ge(h, val)
                if is_da:
                    body(Ctx._Pre(self, key, eng))
                    if mile:
                        dr = eng.drain()
                        self.dirty[key] = False
                        dr.then_inc(mile_sem.h, 1)
                else:
                    rec = Ctx._Rec(eng)
                    body(rec)
                    if mile:
                        last = object.__getattribute__(rec, "last")
                        ins = last if last is not None else eng.nop()
                        ins.then_inc(mile_sem.h, 1)
            self.pend[key].append(emit)
            if len(self.pend[key]) >= 64:
                self._flush_key(key)
            return mval
        return run

    def _flush_key(self, key):
        closures = self.pend[key]
        if not closures:
            return
        self.pend[key] = []
        section = {"v": self.blk.vector, "s": self.blk.scalar, "g": self.blk.gpsimd,
                   "t": self.blk.tensor, "y": self.blk.sync}[key]

        def f(_e):
            for c in closures:
                c()
        section(f)

    def flush(self):
        for key in ["y", "g", "t", "s", "v"]:
            self._flush_key(key)

    def make_steps(self):
        nc, blk = self.nc, self.blk
        V = self._mk(blk.vector, nc.vector, "v", self.sem["vs"])
        S = self._mk(blk.scalar, nc.scalar, "s", self.sem["ac"])
        G = self._mk(blk.gpsimd, nc.gpsimd, "g", self.sem["gp"])
        T_ = self._mk(blk.tensor, nc.tensor, "t", self.sem["pe"])
        Y = self._mk(blk.sync, nc.sync, "y", self.sem["ld"])
        return V, S, G, T_, Y


def _stream_loads(ctx, Y, G, triples, tag):
    gp = ctx.sem["gp"]
    lds = [ctx.new_sem(f"{tag}l{i}") for i in range(2)]
    miles = []
    for k, (src, stg, dst) in enumerate(triples):
        sl = lds[k % 2]
        sl.n += 16
        need = sl.n

        def _y(y, k=k, src=src, stg=stg, sl=sl):
            if k >= 2:
                y.wait_ge(gp.h, miles[k - 2])
            y.dma_start(out=stg, in_=src).then_inc(sl.h, 16)
        Y(_y, mile=False)

        def _g(g, stg=stg, dst=dst, need=need, sl=sl):
            g.wait_ge(sl.h, need)
            g.tensor_copy(dst, stg)
        miles.append(G(_g))
    return miles



def make_ln_scratch(nc, e, tag):
    ln = {}
    ln["sq"] = [e(nc.sbuf_tensor(f"{tag}sq{i}", [P, 512], F32)) for i in range(2)]
    ln["ones"] = e(nc.sbuf_tensor(f"{tag}ones", [P, P], F32))
    ln["m_sb"] = e(nc.sbuf_tensor(f"{tag}m", [1, 512], F32))
    ln["e2_sb"] = e(nc.sbuf_tensor(f"{tag}e2", [1, 512], F32))
    ln["var_sb"] = e(nc.sbuf_tensor(f"{tag}var", [1, 512], F32))
    ln["rstd_sb"] = e(nc.sbuf_tensor(f"{tag}rstd", [1, 512], F32))
    ln["mrstd_sb"] = e(nc.sbuf_tensor(f"{tag}mrstd", [1, 512], F32))
    ln["st_s"] = e(nc.psum_tensor(f"{tag}sts", [1, 512], F32))
    ln["st_q"] = e(nc.psum_tensor(f"{tag}stq", [1, 512], F32))
    ln["bc_r"] = e(nc.psum_tensor(f"{tag}bcr", [P, 512], F32))
    ln["bc_m"] = e(nc.psum_tensor(f"{tag}bcm", [P, 512], F32))
    return ln


def emit_ln(ctx, V, S, T_, r_chunks, ybf_chunks, ln, chunk_waits=None, ybf_waits=None):
    pe, vs, ac = ctx.sem["pe"], ctx.sem["vs"], ctx.sem["ac"]
    n = len(r_chunks)
    mm_m = [None, None]
    for c in range(n):
        w = list(chunk_waits[c]) if chunk_waits else []
        if mm_m[c % 2] is not None:
            w.append((pe, mm_m[c % 2]))
        if c < 2 and ln.get("_last_apply") is not None:
            w.append((vs, ln["_last_apply"]))

        def _s(s, c=c):
            s.activation(ln["sq"][c % 2][:], r_chunks[c], AF.Square)
        sqm = S(_s, waits=w)

        def _t(t, c=c, sqm=sqm):
            t.wait_ge(ac.h, sqm)
            t.matmul(ln["st_s"][:], ln["ones"][:, 0:1], r_chunks[c], start=(c == 0), stop=(c == n - 1))
            t.matmul(ln["st_q"][:], ln["ones"][:, 0:1], ln["sq"][c % 2][:], start=(c == 0), stop=(c == n - 1))
        mm_m[c % 2] = T_(_t)
    stats_done = mm_m[(n - 1) % 2]

    def _v1(v):
        v.tensor_scalar_mul(ln["m_sb"][:], ln["st_s"][:], 1.0 / D)
        v.tensor_scalar_mul(ln["e2_sb"][:], ln["st_q"][:], 1.0 / D)
        v.tensor_tensor(ln["var_sb"][:], ln["m_sb"][:], ln["m_sb"][:], ALU.mult)
        v.tensor_tensor(ln["var_sb"][:], ln["e2_sb"][:], ln["var_sb"][:], ALU.subtract)
        v.tensor_scalar_add(ln["var_sb"][:], ln["var_sb"][:], EPS_LN)
    v1 = V(_v1, waits=[(pe, stats_done)])

    def _s2(s):
        s.activation(ln["rstd_sb"][:], ln["var_sb"][:], AF.Ln)
        s.activation(ln["rstd_sb"][:], ln["rstd_sb"][:], AF.Exp, scale=-0.5)
    s2 = S(_s2, waits=[(vs, v1)])

    def _v2(v):
        v.tensor_tensor(ln["mrstd_sb"][:], ln["m_sb"][:], ln["rstd_sb"][:], ALU.mult)
    v2 = V(_v2, waits=[(ac, s2)])

    def _t2(t):
        t.matmul(ln["bc_r"][:], ln["ones"][0:1, :], ln["rstd_sb"][:], start=True, stop=True)
        t.matmul(ln["bc_m"][:], ln["ones"][0:1, :], ln["mrstd_sb"][:], start=True, stop=True)
    t2 = T_(_t2, waits=[(vs, v2)])

    out_m = []
    for c in range(n):
        w = [(pe, t2)]
        if ybf_waits:
            w += list(ybf_waits[c])

        def _v3(v, c=c):
            v.tensor_tensor(ln["sq"][c % 2][:], r_chunks[c], ln["bc_r"][:], ALU.mult)
            v.tensor_tensor(r_chunks[c], ln["sq"][c % 2][:], ln["bc_m"][:], ALU.subtract)
            if ybf_chunks is not None:
                v.tensor_copy(ybf_chunks[c], r_chunks[c])
        out_m.append(V(_v3, waits=w))
    ln["_last_apply"] = out_m[-1]
    return out_m



def emit_stage1(ctx, d, T):
    nc, e = ctx.nc, ctx.ex.enter_context
    NT = T // P
    xstage = [e(nc.sbuf_tensor(f"s1xs{i}", [P, T], F32)) for i in range(2)]
    x_bf = [e(nc.sbuf_tensor(f"s1xb{i}", [P, T], BF16)) for i in range(8)]
    wstage = [e(nc.sbuf_tensor(f"s1ws{i}", [P, 3072], F32)) for i in range(2)]
    w_bf = [e(nc.sbuf_tensor(f"s1wb{i}", [P, 3072], BF16)) for i in range(8)]
    sqf = e(nc.sbuf_tensor("s1sqf", [P, 512], F32))
    sq16 = e(nc.sbuf_tensor("s1sq16", [P, 8], F32))
    rn16 = e(nc.sbuf_tensor("s1rn16", [P, 8], F32))
    qn = e(nc.sbuf_tensor("s1qn", [P, 512], F32))
    esb = e(nc.sbuf_tensor("s1e", [P, 512], F32))
    qf_sb = [e(nc.sbuf_tensor(f"s1qf{i}", [P, D], BF16)) for i in range(2)]
    kf_sb = e(nc.sbuf_tensor("s1kf", [P, D], BF16))
    vh_sb = e(nc.sbuf_tensor("s1vh", [P, NH * 65], BF16))
    kv2 = e(nc.sbuf_tensor("s1kv2", [P, 2048], F32))
    kv_acc = e(nc.sbuf_tensor("s1kvacc", [64, NH * 65], F32))
    ks16 = e(nc.sbuf_tensor("s1ks16", [64, NH], F32))
    kst = e(nc.sbuf_tensor("s1kst", [NH, 64], F32))
    ksrow = e(nc.sbuf_tensor("s1ksrow", [1, D], F32))
    id_sb = e(nc.sbuf_tensor("s1id", [64, 64], F32))
    qkv_ps = [e(nc.psum_tensor(f"s1qp{i}", [P, 512], F32)) for i in range(2)]
    kv_ps = e(nc.psum_tensor("s1kvp", [64, NH * P], F32))
    tr_ps = e(nc.psum_tensor("s1trp", [NH, 64], F32))

    V, S, G, T_, Y = ctx.make_steps()
    ld, pe, vs, ac, gp, stq = (ctx.sem[k] for k in ["ld", "pe", "vs", "ac", "gp", "stq"])

    ldid = ctx.new_sem("ldid")
    stqf = [ctx.new_sem(f"stqf{i}") for i in range(2)]

    ldid.n += 16

    def _yid(y):
        y.dma_start(out=id_sb[:], in_=d["ident"][0:64, 0:64]).then_inc(ldid.h, 16)
    Y(_yid, mile=False)

    def _vinit(v):
        v.memset(kv_acc[:], 0.0)
        v.memset(kv2[:], 0.0)
    kvacc_init = V(_vinit)

    triples = [(d["wqkvT"][i * P:(i + 1) * P, :], wstage[i % 2][:], w_bf[i][:]) for i in range(8)]
    triples += [(d["xT"][i * P:(i + 1) * P, :], xstage[i % 2][:], x_bf[i][:]) for i in range(8)]
    conv_m = _stream_loads(ctx, Y, G, triples, "s1w")
    all_conv = conv_m[-1]

    bank_free = {}
    qf_store_n = {}
    kv_mm_prev = None
    qf_ready = None
    for tc in range(NT):
        kread = None
        vh_done = None
        for ct in range(6):
            b = ct % 2
            path = "q" if ct < 2 else ("k" if ct < 4 else "v")
            gi = tc * 6 + ct
            w = []
            if gi == 0:
                w.append((gp, all_conv))
            if gi >= 2:
                w.append((vs, bank_free[gi - 2]))

            def _mm(t, tc=tc, ct=ct, b=b):
                for i in range(8):
                    t.matmul(qkv_ps[b][:], x_bf[i][:, tc * P:(tc + 1) * P],
                             w_bf[i][:, ct * 512:(ct + 1) * 512],
                             start=(i == 0), stop=(i == 7))
            pe_mm = T_(_mm, waits=w)

            if path in ("q", "k"):
                def _s1(s, b=b):
                    s.activation(sqf[:], qkv_ps[b][:], AF.Square)
                sw = [(pe, pe_mm)]
                if path == "q" and tc >= 2:
                    sw.append((stqf[tc % 2], qf_store_n[tc - 2]))
                if kv_mm_prev is not None:
                    sw.append((pe, kv_mm_prev))
                a1 = S(_s1, waits=sw)

                def _v1(v):
                    v.reduce_sum(sq16[:], sqf[:].rearrange("p (h z) -> p h z", z=HD), axis=AX.X)
                    v.tensor_scalar_max(sq16[:], sq16[:], L2_EPS2)
                v1 = V(_v1, waits=[(ac, a1)])

                def _s2(s):
                    s.activation(rn16[:], sq16[:], AF.Ln)
                    s.activation(rn16[:], rn16[:], AF.Exp, scale=-0.5)
                a2 = S(_s2, waits=[(vs, v1)])

                def _v2(v, b=b):
                    in0 = qkv_ps[b][:].rearrange("p (h z) -> p h z", z=HD)
                    in1 = rn16[:].rearrange("p (h o) -> p h o", o=1)
                    in1b, in0b = bass.broadcast_tensor_aps(in1, in0)
                    v.tensor_tensor(qn[:].rearrange("p (h z) -> p h z", z=HD), in0b, in1b, ALU.mult)
                v2 = V(_v2, waits=[(ac, a2)])
                bank_free[gi] = v2

                def _s3(s, tc=tc, ct=ct, path=path):
                    s.activation(esb[:], qn[:], AF.Exp)
                    dst = qf_sb[tc % 2] if path == "q" else kf_sb
                    off = (ct % 2) * 512
                    s.activation(dst[:, off:off + 512], esb[:], AF.Ln, bias=1.0)
                a3 = S(_s3, waits=[(vs, v2)])
                if path == "q" and ct == 1:
                    qf_ready = a3
                if path == "k" and ct == 3:
                    kread = a3
            else:
                def _v3(v, b=b, ct=ct, tc=tc):
                    if tc == 0 and ct == 4:
                        for h in range(NH):
                            v.memset(vh_sb[:, h * 65 + 64:h * 65 + 65], 1.0)
                    off_h = (ct - 4) * 8
                    out = vh_sb[:, off_h * 65:(off_h + 8) * 65].rearrange(
                        "p (h z) -> p h z", z=65)[:, :, 0:HD]
                    v.tensor_copy(out, qkv_ps[b][:].rearrange("p (h z) -> p h z", z=HD))
                vw = [(pe, pe_mm)]
                if kv_mm_prev is not None:
                    vw.append((pe, kv_mm_prev))
                v3 = V(_v3, waits=vw)
                bank_free[gi] = v3
                if ct == 5:
                    vh_done = v3

        kvw = [(ac, kread), (vs, vh_done)]
        if tc > 0:
            kvw.append((vs, kvacc_m))
        else:
            kvw.append((vs, kvacc_init))

        def _kv(t, tc=tc):
            for h in range(NH):
                t.matmul(kv_ps[0:64, h * P:h * P + 65],
                         kf_sb[:, h * HD:(h + 1) * HD],
                         vh_sb[:, h * 65:(h + 1) * 65],
                         start=(h % 4 == 0), stop=(h % 4 == 3))
        kv_mm_prev = T_(_kv, waits=kvw)

        def _vkv(v):
            v.tensor_tensor(kv_acc[:].rearrange("p (h z) -> p h z", z=65),
                            kv_acc[:].rearrange("p (h z) -> p h z", z=65),
                            kv_ps[0:64, :].rearrange("p (h z) -> p h z", z=P)[:, :, 0:65],
                            ALU.add)
        kvacc_m = V(_vkv, waits=[(pe, kv_mm_prev)])

        stqf[tc % 2].n += 16
        qf_store_n[tc] = stqf[tc % 2].n

        def _gst(g, tc=tc, qf_ready=qf_ready):
            g.wait_ge(ac.h, qf_ready)
            g.dma_start(out=d["qf_hbm"][tc * P:(tc + 1) * P, :],
                        in_=qf_sb[tc % 2][:]).then_inc(stqf[tc % 2].h, 16)
        G(_gst, mile=False)

    def _vep(v):
        kvv = kv_acc[:].rearrange("p (h z) -> p h z", z=65)
        out3 = kv2[0:64, :].rearrange("p (h z) -> p h z", z=P)
        v.tensor_scalar_mul(out3[:, :, 0:65], kvv, float(HD) ** -0.5)
        v.tensor_scalar_add(out3[:, :, 64:65], kvv[:, :, 64:65], EPS_LA)
        v.tensor_copy(kv2[64:128, :], kv2[0:64, :])
        v.tensor_copy(ks16[:].rearrange("p (h o) -> p h o", o=1), out3[:, :, 64:65])
    vep = V(_vep, waits=[(vs, kvacc_m)])

    def _pet(t):
        t.wait_ge(ldid.h, 16)
        t.transpose(tr_ps[:], ks16[:], id_sb[:])
    pet = T_(_pet, waits=[(vs, vep)])

    def _vep2(v):
        v.tensor_copy(kst[:], tr_ps[:])
    vep2 = V(_vep2, waits=[(pe, pet)])

    n_qf0, n_qf1 = stqf[0].n, stqf[1].n
    stq.n += 48

    def _gep(g, n0=n_qf0, n1=n_qf1, nf=stq.n):
        g.wait_ge(stqf[0].h, n0)
        g.wait_ge(stqf[1].h, n1)
        g.dma_start(out=ksrow[:], in_=kst[:]).then_inc(stq.h, 16)
        g.wait_ge(stq.h, nf - 32)
        g.dma_start(out=d["ksrow_hbm"], in_=ksrow[:]).then_inc(stq.h, 16)
        g.dma_start(out=d["kv2_hbm"], in_=kv2[:]).then_inc(stq.h, 16)
        g.wait_ge(stq.h, nf)
    G(_gep, waits=[(vs, vep2)], mile=False)
    ctx.flush()



def emit_stage2(ctx, d, T):
    nc, e = ctx.nc, ctx.ex.enter_context
    NTile = T // 512
    kv2 = e(nc.sbuf_tensor("s2kv2", [P, 2048], F32))
    kv2b = e(nc.sbuf_tensor("s2kv2b", [P, 2048], BF16))
    ksrow = e(nc.sbuf_tensor("s2ksrow", [1, D], F32))
    ksb = e(nc.sbuf_tensor("s2ksb", [P, D], F32))
    s1c = e(nc.sbuf_tensor("s2s1c", [P, 32], F32))
    s2c = e(nc.sbuf_tensor("s2s2c", [P, 32], F32))
    sh1c = e(nc.sbuf_tensor("s2sh1c", [P, 32], F32))
    sh2c = e(nc.sbuf_tensor("s2sh2c", [P, 32], F32))
    s3c = e(nc.sbuf_tensor("s2s3c", [P, 8], F32))
    qf_in = [e(nc.sbuf_tensor(f"s2qfi{i}", [P, D], BF16)) for i in range(2)]
    prod = e(nc.sbuf_tensor("s2prod", [P, D], F32))
    den16 = e(nc.sbuf_tensor("s2den", [P, NH], F32))
    qfs_o = [e(nc.sbuf_tensor(f"s2qfs{i}", [P, D], BF16)) for i in range(2)]
    qfsT = [e(nc.sbuf_tensor(f"s2qT{i}", [P, 512], BF16)) for i in range(8)]
    xres = [e(nc.sbuf_tensor(f"s2xr{i}", [P, 512], F32)) for i in range(2)]
    y1f = [e(nc.sbuf_tensor(f"s2y1f{i}", [P, 512], F32)) for i in range(8)]
    y1b = [e(nc.sbuf_tensor(f"s2y1b{i}", [P, 512], BF16)) for i in range(8)]
    r2 = [e(nc.sbuf_tensor(f"s2r2_{i}", [P, 512], F32)) for i in range(8)]
    y2b = [e(nc.sbuf_tensor(f"s2y2b{i}", [P, 512], BF16)) for i in range(8)]
    abuf = [e(nc.sbuf_tensor(f"s2a{i}", [P, 2048], BF16)) for i in range(8)]
    bbuf = [e(nc.sbuf_tensor(f"s2b{i}", [P, 2048], BF16)) for i in range(8)]
    u1scr = [e(nc.sbuf_tensor(f"s2u1{i}", [P, 512], F32)) for i in range(2)]
    wstg = [e(nc.sbuf_tensor(f"s2ws{i}", [P, 1024], F32)) for i in range(2)]
    wbf = [e(nc.sbuf_tensor(f"s2wb{i}", [P, 1024], BF16)) for i in range(2)]
    ln = make_ln_scratch(nc, e, "s2ln")
    mm_ps = [e(nc.psum_tensor(f"s2mp{i}", [P, 512], F32)) for i in range(2)]

    V, S, G, T_, Y = ctx.make_steps()
    ld, pe, vs, ac, gp, stq = (ctx.sem[k] for k in ["ld", "pe", "vs", "ac", "gp", "stq"])
    ldw = [ctx.new_sem(f"ldw{i}") for i in range(2)]
    ldq = [ctx.new_sem(f"ldq{i}") for i in range(2)]
    ldx = [ctx.new_sem(f"ldx{i}") for i in range(2)]
    ldt = ctx.new_sem("ldt")
    ldpre = ctx.new_sem("ldpre")
    stqf = [ctx.new_sem(f"s2stqf{i}") for i in range(2)]
    stout = ctx.new_sem("stout")

    def a_chunk(m):
        return abuf[m // 4][:, (m % 4) * 512:(m % 4) * 512 + 512]

    def b_chunk(m):
        return bbuf[m // 4][:, (m % 4) * 512:(m % 4) * 512 + 512]

    ldpre.n += 7 * 16
    pre_ld = ldpre.n

    def _y0(y):
        for src, dst in [(d["kv2_hbm"], kv2[:]), (d["ksrow_hbm"], ksrow[:]),
                         (d["s1c"], s1c[:]), (d["s2c"], s2c[:]), (d["sh1c"], sh1c[:]),
                         (d["sh2c"], sh2c[:]), (d["s3c"], s3c[:])]:
            y.dma_start(out=dst, in_=src).then_inc(ldpre.h, 16)
    Y(_y0, mile=False)

    def _v0(v):
        v.memset(ln["ones"][:], 1.0)
    v0 = V(_v0)

    def _g0(g):
        g.wait_ge(ldpre.h, pre_ld)
        g.tensor_copy(kv2b[:], kv2[:])
    g0 = G(_g0)

    def _t0(t):
        t.wait_ge(ldpre.h, pre_ld)
        t.matmul(ln["bc_r"][:, 0:512], ln["ones"][0:1, :], ksrow[0:1, 0:512], start=True, stop=True)
        t.matmul(ln["bc_m"][:, 0:512], ln["ones"][0:1, :], ksrow[0:1, 512:1024], start=True, stop=True)
    t0 = T_(_t0, waits=[(vs, v0)])

    def _v01(v):
        v.tensor_copy(ksb[:, 0:512], ln["bc_r"][:, 0:512])
        v.tensor_copy(ksb[:, 512:1024], ln["bc_m"][:, 0:512])
    v01 = V(_v01, waits=[(pe, t0)])

    wconv = [None, None]
    wuse = [None, None]

    def queue_wload(src_ap, ncols, slot):
        ldw[slot].n += 16
        need = ldw[slot].n

        def _yw(y, src_ap=src_ap, ncols=ncols, slot=slot, wc=wconv[slot]):
            if wc is not None:
                y.wait_ge(gp.h, wc)
            y.dma_start(out=wstg[slot][:, 0:ncols], in_=src_ap).then_inc(ldw[slot].h, 16)
        Y(_yw, mile=False)

        def _gw(g, ncols=ncols, slot=slot, need=need, wu=wuse[slot]):
            g.wait_ge(ldw[slot].h, need)
            if wu is not None:
                g.wait_ge(pe.h, wu)
            g.tensor_copy(wbf[slot][:, 0:ncols], wstg[slot][:, 0:ncols])
        wconv[slot] = G(_gw)
        return wconv[slot]

    qf_used = {}
    qfs_done = {}
    xres_used = {}
    latt_used = {}
    evac_free = [None, None]
    y2b_store = {}
    r2_store = {}
    glob_j = 0

    for ti in range(NTile):
        for j in range(4):
            tc = ti * 4 + j
            gj = glob_j

            ldq[gj % 2].n += 16
            qld = ldq[gj % 2].n

            def _yq(y, tc=tc, gj=gj):
                if gj >= 2:
                    y.wait_ge(vs.h, qf_used[gj - 2])
                y.dma_start(out=qf_in[gj % 2][:],
                            in_=d["qf_hbm"][tc * P:(tc + 1) * P, :]).then_inc(ldq[gj % 2].h, 16)
            Y(_yq, mile=False)

            w = [(ldq[gj % 2], qld)]
            if gj == 0:
                w.append((vs, v01))
            if gj >= 2:
                w.append(qfs_done[gj - 2])

            def _vq(v, gj=gj):
                v.tensor_tensor(prod[:], qf_in[gj % 2][:], ksb[:], ALU.mult)
                v.reduce_sum(den16[:], prod[:].rearrange("p (h z) -> p h z", z=HD), axis=AX.X)
                v.reciprocal(den16[:], den16[:])
                in0 = qf_in[gj % 2][:].rearrange("p (h z) -> p h z", z=HD)
                in1 = den16[:].rearrange("p (h o) -> p h o", o=1)
                in1b, in0b = bass.broadcast_tensor_aps(in1, in0)
                v.tensor_tensor(qfs_o[gj % 2][:].rearrange("p (h z) -> p h z", z=HD),
                                in0b, in1b, ALU.mult)
            vq = V(_vq, waits=w)
            qf_used[gj] = vq

            stqf[gj % 2].n += 16
            qfs_done[gj] = (stqf[gj % 2], stqf[gj % 2].n)

            def _gq(g, tc=tc, gj=gj, vq=vq):
                g.wait_ge(vs.h, vq)
                g.dma_start(out=d["qfs_hbm"][tc * P:(tc + 1) * P, :],
                            in_=qfs_o[gj % 2][:]).then_inc(stqf[gj % 2].h, 16)
            G(_gq, mile=False)
            glob_j += 1
        all_qfs = [(stqf[0], stqf[0].n), (stqf[1], stqf[1].n)]

        ldt.n += 8 * 16
        t_ld = ldt.n

        def _yt(y, ti=ti, all_qfs=all_qfs):
            for sem, val in all_qfs:
                y.wait_ge(sem.h, val)
            if ti > 0:
                y.wait_ge(pe.h, latt_used[ti - 1])
            for c in range(8):
                y.dma_start(out=qfsT[c][:],
                            in_=d["qfs_hbm"][ti * 512:(ti + 1) * 512, c * P:(c + 1) * P],
                            transpose=True).then_inc(ldt.h, 16)
        Y(_yt, mile=False)

        y1_ready = {}
        for cp in range(8):
            ldx[cp % 2].n += 16
            x_ld = ldx[cp % 2].n

            def _yx(y, cp=cp, ti=ti):
                k = ti * 8 + cp
                if k >= 2:
                    y.wait_ge(vs.h, xres_used[k - 2])
                y.dma_start(out=xres[cp % 2][:],
                            in_=d["xT"][cp * P:(cp + 1) * P, ti * 512:(ti + 1) * 512]
                            ).then_inc(ldx[cp % 2].h, 16)
            Y(_yx, mile=False)

            w = [(ldt, t_ld)]
            if evac_free[cp % 2] is not None:
                w.append(evac_free[cp % 2])

            def _tl(t, cp=cp):
                for par in range(2):
                    h = 2 * cp + par
                    bs = (h % 2) * 64
                    t.matmul(mm_ps[cp % 2][bs:bs + 64, :],
                             kv2b[bs:bs + 64, h * P:h * P + 64],
                             qfsT[cp][bs:bs + 64, :],
                             start=True, stop=True)
            if ti == 0 and cp == 0:
                w.append((gp, g0))
            tl = T_(_tl, waits=w)

            def _vr(v, cp=cp, x_ld=x_ld, tl=tl):
                v.wait_ge(ldx[cp % 2].h, x_ld)
                v.wait_ge(pe.h, tl)
                v.tensor_tensor(y1f[cp][:], mm_ps[cp % 2][:], xres[cp % 2][:], ALU.add)
            vr = V(_vr, waits=[(vs, ln1_done[cp]) if ti > 0 else (vs, None)])
            evac_free[cp % 2] = (vs, vr)
            xres_used[ti * 8 + cp] = vr
            y1_ready[cp] = vr
        latt_used[ti] = pe.n

        cw = [[(vs, y1_ready[c])] for c in range(8)]
        bw = [[(pe, gate_done)] if ti > 0 else [] for c in range(8)]
        ln1_done = emit_ln(ctx, V, S, T_,
                           [y1f[c][:] for c in range(8)],
                           [y1b[c][:] for c in range(8)],
                           ln, chunk_waits=cw, ybf_waits=bw)

        u1_used = [None, None]
        a_done = {}
        for m in range(32):
            c1 = queue_wload(d["w1Tr"][:, :, m * P:(m + 1) * P], 1024, 0)

            w = [(gp, c1)]
            if evac_free[0] is not None:
                w.append(evac_free[0])
            if m == 0:
                w += [(vs, x) for x in ln1_done]

            def _t1(t):
                for i in range(8):
                    t.matmul(mm_ps[0][:], wbf[0][:, i * P:(i + 1) * P], y1b[i][:],
                             start=(i == 0), stop=(i == 7))
            t1 = T_(_t1, waits=w)
            wuse[0] = t1

            w = [(pe, t1)]
            if u1_used[m % 2] is not None:
                w.append((vs, u1_used[m % 2]))

            def _s1(s, m=m):
                s.activation(u1scr[m % 2][:], mm_ps[0][:], AF.Silu, scale=s1c[:, m:m + 1])
            ev1 = S(_s1, waits=w)
            evac_free[0] = (ac, ev1)

            c2 = queue_wload(d["w2Tr"][:, :, m * P:(m + 1) * P], 1024, 1)
            w = [(gp, c2)]
            if evac_free[1] is not None:
                w.append(evac_free[1])

            def _t2(t):
                for i in range(8):
                    t.matmul(mm_ps[1][:], wbf[1][:, i * P:(i + 1) * P], y1b[i][:],
                             start=(i == 0), stop=(i == 7))
            t2 = T_(_t2, waits=w)
            wuse[1] = t2

            def _v2(v, m=m):
                v.scalar_tensor_tensor(a_chunk(m), mm_ps[1][:], s2c[:, m:m + 1],
                                       u1scr[m % 2][:], ALU.mult, ALU.mult)
            v2 = V(_v2, waits=[(pe, t2), (ac, ev1)])
            evac_free[1] = (vs, v2)
            u1_used[m % 2] = v2
            a_done[m] = v2
        gate_done = pe.n

        c_all_done = None
        for layer in range(2):
            wname = "h1Tr" if layer == 0 else "h2Tr"
            scol = sh1c if layer == 0 else sh2c
            prev_evac = evac_free[0] if layer == 0 else h1_all
            for m in range(32):
                mm_here = None
                for sub in range(4):
                    csub = queue_wload(d[wname][:, 8 * sub:8 * sub + 8, m * P:(m + 1) * P], 1024, sub % 2)
                    w = [(gp, csub)]
                    if sub == 0:
                        if prev_evac is not None:
                            w.append(prev_evac)
                        if m == 0 and layer == 0:
                            w.append((vs, a_done[31]))

                    def _tm(t, sub=sub, layer=layer):
                        for i in range(8):
                            t.matmul(mm_ps[0][:], wbf[sub % 2][:, i * P:(i + 1) * P],
                                     (a_chunk(8 * sub + i) if layer == 0 else b_chunk(8 * sub + i)),
                                     start=(sub == 0 and i == 0), stop=(sub == 3 and i == 7))
                    mm_here = T_(_tm, waits=w)
                    wuse[sub % 2] = mm_here

                def _se(s, m=m, layer=layer, scol=scol):
                    s.activation((b_chunk(m) if layer == 0 else a_chunk(m)),
                                 mm_ps[0][:], AF.Gelu, scale=scol[:, m:m + 1])
                ev = S(_se, waits=[(pe, mm_here)])
                prev_evac = (ac, ev)
            if layer == 0:
                h1_all = prev_evac
            else:
                c_all_done = prev_evac

        prev_evac4 = None
        r2_ready = {}
        for m in range(8):
            mm_here = None
            for sub in range(4):
                csub = queue_wload(d["w3Tr"][:, 8 * sub:8 * sub + 8, m * P:(m + 1) * P], 1024, sub % 2)
                w = [(gp, csub)]
                if sub == 0:
                    if prev_evac4 is not None:
                        w.append(prev_evac4)
                    if m == 0:
                        w.append(c_all_done)

                def _tm(t, sub=sub):
                    for i in range(8):
                        t.matmul(mm_ps[0][:], wbf[sub % 2][:, i * P:(i + 1) * P],
                                 a_chunk(8 * sub + i),
                                 start=(sub == 0 and i == 0), stop=(sub == 3 and i == 7))
                mm_here = T_(_tm, waits=w)
                wuse[sub % 2] = mm_here

            w = [(pe, mm_here)]
            if ti > 0:
                w.append((stout, prev_tile_stores))

            def _ve(v, m=m):
                v.scalar_tensor_tensor(r2[m][:], mm_ps[0][:], s3c[:, m:m + 1],
                                       y1f[m][:], ALU.mult, ALU.add)
            ev = V(_ve, waits=w)
            prev_evac4 = (vs, ev)
            r2_ready[m] = ev
        evac_free[0] = prev_evac4

        cw = [[(vs, r2_ready[c])] for c in range(8)]
        bw = [[(stout, prev_tile_stores)] if ti > 0 else [] for c in range(8)]
        ln2_done = emit_ln(ctx, V, S, T_,
                           [r2[c][:] for c in range(8)],
                           [y2b[c][:] for c in range(8)],
                           ln, chunk_waits=cw, ybf_waits=bw)

        for c in range(8):
            stout.n += 32

            def _gst(g, c=c, ti=ti):
                g.wait_ge(vs.h, ln2_done[c])
                g.dma_start(out=d["y2f_hbm"][c * P:(c + 1) * P, ti * 512:(ti + 1) * 512],
                            in_=r2[c][:]).then_inc(stout.h, 16)
                g.dma_start(out=d["y2b_hbm"][c * P:(c + 1) * P, ti * 512:(ti + 1) * 512],
                            in_=y2b[c][:]).then_inc(stout.h, 16)
            G(_gst, mile=False)
        prev_tile_stores = stout.n

    def _gfin(g, a=stout.n, b=stqf[0].n, c=stqf[1].n):
        g.wait_ge(stout.h, a)
        g.wait_ge(stqf[0].h, b)
        g.wait_ge(stqf[1].h, c)
    G(_gfin, mile=False)
    ctx.flush()




def emit_stage3(ctx, d, T):
    nc, e = ctx.nc, ctx.ex.enter_context
    NT2 = T // 512
    wstage = [e(nc.sbuf_tensor(f"s3ws{i}", [P, D], F32)) for i in range(2)]
    weffbf = [[e(nc.sbuf_tensor(f"s3we{p}_{i}", [P, D], BF16)) for i in range(8)] for p in range(3)]
    wefftmp = e(nc.sbuf_tensor("s3wet", [P, D], BF16))
    a_sb = [e(nc.sbuf_tensor(f"s3a{p}", [8, D], F32)) for p in range(4)]
    bt_sb = [e(nc.sbuf_tensor(f"s3bt{p}", [8, D], F32)) for p in range(4)]
    y2t = [e(nc.sbuf_tensor(f"s3y2{i}", [P, 512], BF16)) for i in range(8)]
    qk_bf = [e(nc.sbuf_tensor(f"s3qk{i}", [P, 512], BF16)) for i in range(2)]
    qsq_bf = [e(nc.sbuf_tensor(f"s3qs{i}", [P, 512], BF16)) for i in range(2)]
    hpat = [e(nc.sbuf_tensor(f"s3hp{i}", [P, NH], BF16)) for i in range(8)]
    mx16 = e(nc.sbuf_tensor("s3mx", [NH, 2], F32))
    tmp16 = e(nc.sbuf_tensor("s3tq", [NH, 1], F32))
    bq8 = e(nc.sbuf_tensor("s3bq8", [P, 8], F32))
    bkc = e(nc.sbuf_tensor("s3bkc", [P, 8], F32))
    bvrow = e(nc.sbuf_tensor("s3bvr", [1, D], F32))
    bvb = e(nc.sbuf_tensor("s3bvb", [P, D], F32))
    vh_sb = [e(nc.sbuf_tensor(f"s3vh{i}", [P, NH * 65], BF16)) for i in range(2)]
    id_sb = e(nc.sbuf_tensor("s3id", [NH, NH], F32))
    ones_sb = e(nc.sbuf_tensor("s3on", [1, P], F32))
    negarow = e(nc.sbuf_tensor("s3neg", [1, NH], F32))
    lora_ps = e(nc.psum_tensor("s3lp", [P, D], F32))
    qk_ps = [e(nc.psum_tensor(f"s3qp{i}", [P, 512], F32)) for i in range(2)]
    qs_ps = [e(nc.psum_tensor(f"s3sp{i}", [NH, 512], F32)) for i in range(2)]
    v_ps = [e(nc.psum_tensor(f"s3vp{i}", [P, 512], F32)) for i in range(2)]

    V, S, G, T_, Y = ctx.make_steps()
    ld, pe, vs, ac, gp, stq = (ctx.sem[k] for k in ["ld", "pe", "vs", "ac", "gp", "stq"])
    ldpre = ctx.new_sem("s3pre")
    ldw = [ctx.new_sem(f"s3lw{i}") for i in range(2)]
    ldy = ctx.new_sem("s3ly")
    stqk = [ctx.new_sem(f"s3sq{i}") for i in range(2)]
    stv = [ctx.new_sem(f"s3sv{i}") for i in range(2)]
    names = ["q", "k", "v", "o"]

    ldpre.n += 12 * 16
    pre_n = ldpre.n

    def _y0(y):
        for p in range(4):
            y.dma_start(out=a_sb[p][:], in_=d["a_" + names[p]]).then_inc(ldpre.h, 16)
            y.dma_start(out=bt_sb[p][:], in_=d["bt_" + names[p]]).then_inc(ldpre.h, 16)
        for src, dst in [(d["bq_c"], bq8[:]), (d["bk_c"], bkc[:]),
                         (d["bv_row"], bvrow[:]), (d["ident"][0:NH, 0:NH], id_sb[:])]:
            y.dma_start(out=dst, in_=src).then_inc(ldpre.h, 16)
    Y(_y0, mile=False)

    def _v0(v):
        v.wait_ge(ldpre.h, pre_n)
        v.tensor_scalar_mul(bq8[:], bq8[:], ATT_SCALE)
        v.memset(ones_sb[:], 1.0)
        v.memset(mx16[:], 0.0)
        for c in range(8):
            v.memset(hpat[c][:], 0.0)
        for i in range(2):
            v.memset(vh_sb[i][:].rearrange("p (h z) -> p h z", z=65)[:, :, 64:65], 1.0)
        for c in range(8):
            v.memset(hpat[c][0:64, 2 * c:2 * c + 1], 1.0)
            v.memset(hpat[c][64:128, 2 * c + 1:2 * c + 2], 1.0)
    v0 = V(_v0)

    evac_prev = {}
    for p in range(4):
        for i in range(8):
            k = p * 8 + i
            slot = k % 2

            ldw[slot].n += 16
            wn = ldw[slot].n

            def _yw(y, p=p, i=i, slot=slot, k=k, ep=evac_prev.get(k - 2)):
                if k >= 2 and ep is not None:
                    y.wait_ge(vs.h, ep)
                y.dma_start(out=wstage[slot][:],
                            in_=d["w_" + names[p]][i * P:(i + 1) * P, :]).then_inc(ldw[slot].h, 16)
            Y(_yw, mile=False)

            w = [(ldpre, pre_n)] if k == 0 else []
            if k >= 1:
                w.append((vs, evac_prev.get(k - 1)))

            def _tm(t, p=p, i=i):
                for m in range(8):
                    t.matmul(lora_ps[:, m * P:(m + 1) * P],
                             a_sb[p][0:8, i * P:(i + 1) * P],
                             bt_sb[p][0:8, m * P:(m + 1) * P],
                             start=True, stop=True)
            tm = T_(_tm, waits=w)

            def _ve(v, p=p, i=i, slot=slot, wn=wn, tm=tm):
                v.wait_ge(ldw[slot].h, wn)
                v.wait_ge(pe.h, tm)
                dst = weffbf[p][i][:] if p < 3 else wefftmp[:]
                v.scalar_tensor_tensor(dst, lora_ps[:], 2.0, wstage[slot][:],
                                       ALU.mult, ALU.add)
            ve = V(_ve)
            evac_prev[k] = ve

            if p == 3:
                stq.n += 16

                def _go(g, i=i, ve=ve):
                    g.wait_ge(vs.h, ve)
                    g.dma_start(out=d["weffoT_hbm"][i * P:(i + 1) * P, :],
                                in_=wefftmp[:]).then_inc(stq.h, 16)
                G(_go, mile=False)
                evac_prev[k] = None

                def _vg(v, n=stq.n):
                    v.wait_ge(stq.h, n)
                vgm = V(_vg)
                evac_prev[k] = vgm

    def _tb(t):
        t.matmul(lora_ps[:, 0:512], ones_sb[0:1, 0:P], bvrow[0:1, 0:512], start=True, stop=True)
        t.matmul(lora_ps[:, 512:1024], ones_sb[0:1, 0:P], bvrow[0:1, 512:1024], start=True, stop=True)
    tb = T_(_tb, waits=[(vs, evac_prev[31])])

    def _vb(v):
        v.tensor_copy(bvb[:], lora_ps[:])
    vb = V(_vb, waits=[(pe, tb)])

    y2_used = None
    sq_prev = {}
    th_prev = {}
    st_prev = {}
    stats_done = {0: None, 1: None}
    v_bank = {}
    vh_guard = [None, None]
    for tt in range(NT2):
        ldy.n += 8 * 16
        yn = ldy.n

        def _yy(y, tt=tt, yu=y2_used):
            if yu is not None:
                y.wait_ge(pe.h, yu)
            for i in range(8):
                y.dma_start(out=y2t[i][:],
                            in_=d["y2b_hbm"][i * P:(i + 1) * P, tt * 512:(tt + 1) * 512]
                            ).then_inc(ldy.h, 16)
        Y(_yy, mile=False)

        for pk in range(2):
            wname = weffbf[pk]
            th = None
            for m in range(8):
                pb = m % 2
                w = [(ldy, yn)]
                if (tt, pk, m) == (0, 0, 0):
                    w.append((vs, v0))
                if sq_prev.get(pb) is not None:
                    w.append((vs, sq_prev[pb][0]))
                    w.append((ac, sq_prev[pb][1]))

                def _tq(t, pk=pk, m=m, pb=pb, wname=wname):
                    for i in range(8):
                        t.matmul(qk_ps[pb][:], wname[i][:, m * P:(m + 1) * P], y2t[i][:],
                                 start=(i == 0), stop=(i == 7))
                tq = T_(_tq, waits=w)

                w = [(pe, tq)]
                if th_prev.get(pb) is not None:
                    w.append((pe, th_prev[pb]))
                if st_prev.get(pb) is not None:
                    w.append(st_prev[pb])

                def _vq2(v, pk=pk, m=m, pb=pb):
                    bias = bq8 if pk == 0 else bkc
                    sc = ATT_SCALE if pk == 0 else 1.0
                    v.tensor_scalar(qk_bf[pb][:], qk_ps[pb][:], sc, bias[:, m:m + 1],
                                    ALU.mult, ALU.add)
                vq2 = V(_vq2, waits=w)

                def _sq(s, pb=pb):
                    s.activation(qsq_bf[pb][:], qk_bf[pb][:], AF.Square)
                sq = S(_sq, waits=[(vs, vq2)])
                sq_prev[pb] = (vq2, sq)

                stqk[pb].n += 16
                st_prev[pb] = (stqk[pb], stqk[pb].n)

                def _gq(g, pk=pk, m=m, tt=tt, pb=pb, vq2=vq2):
                    g.wait_ge(vs.h, vq2)
                    dst = d["q_hbm"] if pk == 0 else d["k_hbm"]
                    g.dma_start(out=dst[m * P:(m + 1) * P, tt * 512:(tt + 1) * 512],
                                in_=qk_bf[pb][:]).then_inc(stqk[pb].h, 16)
                G(_gq, mile=False)

                w = [(ac, sq)]
                if m == 0 and stats_done[pk] is not None:
                    w.append((vs, stats_done[pk]))

                def _th(t, pk=pk, m=m, pb=pb):
                    t.matmul(qs_ps[pk][:], hpat[m][:], qsq_bf[pb][:],
                             start=(m == 0), stop=(m == 7))
                th = T_(_th, waits=w)
                th_prev[pb] = th

            def _vm(v, pk=pk, th=th):
                v.reduce_max(tmp16[:], qs_ps[pk][:], axis=AX.X)
                v.tensor_tensor(mx16[:, pk:pk + 1], mx16[:, pk:pk + 1], tmp16[:], ALU.max)
            stats_done[pk] = V(_vm, waits=[(pe, th)])

        for tsub in range(4):
            vv = None
            for ct in range(2):
                pb = ct
                w = []
                if v_bank.get(pb) is not None:
                    w.append((vs, v_bank[pb]))
                if (tt, tsub, ct) == (0, 0, 0):
                    w.append((vs, vb))
                w.append((ldy, yn))

                def _tv(t, tsub=tsub, ct=ct, pb=pb):
                    for i in range(8):
                        t.matmul(v_ps[pb][:], y2t[i][:, tsub * P:(tsub + 1) * P],
                                 weffbf[2][i][:, ct * 512:(ct + 1) * 512],
                                 start=(i == 0), stop=(i == 7))
                tv = T_(_tv, waits=w)

                w = [(pe, tv)]
                if ct == 0 and vh_guard[tsub % 2] is not None:
                    w.append(vh_guard[tsub % 2])

                def _vv(v, tsub=tsub, ct=ct, pb=pb):
                    dst = vh_sb[tsub % 2][:].rearrange("p (h z) -> p h z", z=65)[:, ct * 8:(ct + 1) * 8, 0:HD]
                    v.tensor_tensor(dst, v_ps[pb][:].rearrange("p (h z) -> p h z", z=HD),
                                    bvb[:, ct * 512:(ct + 1) * 512].rearrange("p (h z) -> p h z", z=HD),
                                    ALU.add)
                vv = V(_vv, waits=w)
                v_bank[pb] = vv

            stv[tsub % 2].n += 16
            vh_guard[tsub % 2] = (stv[tsub % 2], stv[tsub % 2].n)

            def _gv(g, tt=tt, tsub=tsub, vv=vv):
                g.wait_ge(vs.h, vv)
                t0 = tt * 512 + tsub * P
                g.dma_start(out=d["vhat_hbm"][:, t0:t0 + P, :].rearrange("h t f -> t h f"),
                            in_=vh_sb[tsub % 2][:].rearrange("p (h z) -> p h z", z=65)
                            ).then_inc(stv[tsub % 2].h, 16)
            G(_gv, mile=False)
        y2_used = pe.n

    def _vc(v):
        v.tensor_tensor(tmp16[:], mx16[:, 0:1], mx16[:, 1:2], ALU.mult)
        v.tensor_scalar_max(tmp16[:], tmp16[:], 1e-30)
    vc = V(_vc, waits=[(vs, stats_done[0]), (vs, stats_done[1])])

    def _sc(s):
        s.activation(tmp16[:], tmp16[:], AF.Ln)
        s.activation(tmp16[:], tmp16[:], AF.Exp, scale=0.5)
    sc = S(_sc, waits=[(vs, vc)])

    def _vn(v):
        v.tensor_scalar_mul(tmp16[:], tmp16[:], -1.0)
    vn = V(_vn, waits=[(ac, sc)])

    def _tc(t):
        t.transpose(qs_ps[0][0:1, 0:NH], tmp16[:], id_sb[:])
    tc = T_(_tc, waits=[(vs, vn)])

    def _vn2(v):
        v.tensor_copy(negarow[:], qs_ps[0][0:1, 0:NH])
    vn2 = V(_vn2, waits=[(pe, tc)])

    stq.n += 16

    def _gf(g, nf=stq.n, fins=[(sm, sm.n) for sm in [stqk[0], stqk[1], stv[0], stv[1]]]):
        g.dma_start(out=d["negarow_hbm"], in_=negarow[:]).then_inc(stq.h, 16)
        g.wait_ge(stq.h, nf)
        for sm, val in fins:
            g.wait_ge(sm.h, val)
    G(_gf, waits=[(vs, vn2)], mile=False)
    ctx.flush()




def emit_stage4(ctx, d, T):
    nc, e = ctx.nc, ctx.ex.enter_context
    NQO = T // 1024
    NKC = T // P
    NKQ = max(1, T // 1024)
    NKT = min(4, NKQ)
    q_sb = [e(nc.sbuf_tensor(f"s4q{i}", [P, 512], BF16)) for i in range(8)]
    ktile4 = [e(nc.sbuf_tensor(f"s4k{i}", [P, 1024], BF16)) for i in range(NKT)]
    vh_t = [e(nc.sbuf_tensor(f"s4v{i}", [P, NKC * 65], BF16)) for i in range(2)]
    pp_bf = [e(nc.sbuf_tensor(f"s4pp{i}", [P, 512], BF16)) for i in range(4)]
    att_bf = [e(nc.sbuf_tensor(f"s4at{i}", [P, 512], BF16)) for i in range(8)]
    rr_sb = e(nc.sbuf_tensor("s4rr", [1, 512], F32))
    rb_sb = e(nc.sbuf_tensor("s4rb", [64, 512], F32))
    acol = e(nc.sbuf_tensor("s4ac", [P, NH], F32))
    y3f = [e(nc.sbuf_tensor(f"s4y3f{i}", [P, 1024], F32)) for i in range(8)]
    y3b = [e(nc.sbuf_tensor(f"s4y3b{i}", [P, 1024], BF16)) for i in range(8)]
    y2res = [e(nc.sbuf_tensor(f"s4y2r{i}", [P, 512], F32)) for i in range(2)]
    y3scr = e(nc.sbuf_tensor("s4y3s", [P, 512], F32))
    act_e = [e(nc.sbuf_tensor(f"s4ae{i}", [P, 2048], BF16)) for i in range(8)]
    tmp_f = e(nc.sbuf_tensor("s4tmp", [P, 512], F32))
    u1scr = [e(nc.sbuf_tensor(f"s4u1{i}", [P, 512], F32)) for i in range(2)]
    wstg = [e(nc.sbuf_tensor(f"s4ws{i}", [P, D], F32)) for i in range(2)]
    wbf = [e(nc.sbuf_tensor(f"s4wb{i}", [P, D], BF16)) for i in range(2)]
    gw_bf = [e(nc.sbuf_tensor(f"s4gw{i}", [P, 8], BF16)) for i in range(8)]
    gwstg = e(nc.sbuf_tensor("s4gws", [P, 8], F32))
    gb_sb = e(nc.sbuf_tensor("s4gbs", [8, 1], F32))
    bo_c = e(nc.sbuf_tensor("s4bo", [P, 8], F32))
    b0c_sb = [e(nc.sbuf_tensor(f"s4b0_{i}", [P, 16], F32)) for i in range(NE)]
    b1c_sb = [e(nc.sbuf_tensor(f"s4b1_{i}", [P, 16], F32)) for i in range(NE)]
    b2stg = e(nc.sbuf_tensor("s4b2s", [8, D], F32))
    b2_sb = e(nc.sbuf_tensor("s4b2", [8, D], BF16))
    lg_sb = e(nc.sbuf_tensor("s4lg", [8, 512], F32))
    lt_sb = e(nc.sbuf_tensor("s4lt", [P, 8], F32))
    m1_sb = e(nc.sbuf_tensor("s4m1", [P, 1], F32))
    nm1_sb = e(nc.sbuf_tensor("s4nm1", [P, 1], F32))
    m2_sb = e(nc.sbuf_tensor("s4m2", [P, 1], F32))
    msk_sb = e(nc.sbuf_tensor("s4msk", [P, 8], F32))
    rden_sb = e(nc.sbuf_tensor("s4rd", [P, 1], F32))
    g_tm = e(nc.sbuf_tensor("s4gtm", [P, 8], F32))
    g_sb = e(nc.sbuf_tensor("s4gs", [8, 1024], F32))
    g_bf = e(nc.sbuf_tensor("s4gbf", [8, 1024], BF16))
    gflat = e(nc.sbuf_tensor("s4gf", [65, 4096], F32))
    gbv_sb = e(nc.sbuf_tensor("s4gbv", [P, 512], F32))
    nrow_sb = e(nc.sbuf_tensor("s4nr", [1, NH], F32))
    id_sb = e(nc.sbuf_tensor("s4id", [P, P], F32))
    ln = make_ln_scratch(nc, e, "s4ln")
    pa = [e(nc.psum_tensor(f"s4pa{i}", [P, 512], F32)) for i in range(2)]
    pb = [e(nc.psum_tensor(f"s4pb{i}", [65, 512], F32)) for i in range(2)]
    sp = [pa[0], pa[1], ln["bc_r"]]
    aux = ln["bc_m"]

    V, S, G, T_, Y = ctx.make_steps()
    ld, pe, vs, ac, gp, stq = (ctx.sem[k] for k in ["ld", "pe", "vs", "ac", "gp", "stq"])
    ldpre = ctx.new_sem("s4pre")
    ldq = ctx.new_sem("s4lq")
    ldk = [ctx.new_sem(f"s4lk{i}") for i in range(4)]
    ldv = [ctx.new_sem(f"s4lv{i}") for i in range(2)]
    ldy2 = [ctx.new_sem(f"s4ly{i}") for i in range(2)]
    ldw = [ctx.new_sem(f"s4lw{i}") for i in range(2)]
    gfq = ctx.new_sem("s4gfq")
    stout = ctx.new_sem("s4st")

    ldpre.n += (5 + 2 * NE) * 16
    pre_n = ldpre.n

    def _y0(y):
        for src, dst in [(d["negarow_hbm"], nrow_sb[:]), (d["ident"], id_sb[:]),
                         (d["bo_c"], bo_c[:]), (d["gb_c"], gb_sb[:]), (d["b2m"], b2stg[:])]:
            y.dma_start(out=dst, in_=src).then_inc(ldpre.h, 16)
        for ee in range(NE):
            y.dma_start(out=b0c_sb[ee][:], in_=d["b0c"][ee]).then_inc(ldpre.h, 16)
            y.dma_start(out=b1c_sb[ee][:], in_=d["b1c"][ee]).then_inc(ldpre.h, 16)
    Y(_y0, mile=False)

    def _g0(g):
        g.wait_ge(ldpre.h, pre_n)
        g.tensor_copy(b2_sb[:], b2stg[:])
    g0 = G(_g0)

    ldgw = ctx.new_sem("s4lgw")
    gwc = []
    for i in range(8):
        ldgw.n += 16
        nn = ldgw.n

        def _yg(y, i=i):
            if i >= 1:
                y.wait_ge(gp.h, gwc[i - 1])
            y.dma_start(out=gwstg[:], in_=d["gwT"][i * P:(i + 1) * P, :]).then_inc(ldgw.h, 16)
        Y(_yg, mile=False)

        def _gg(g, i=i, nn=nn):
            g.wait_ge(ldgw.h, nn)
            g.tensor_copy(gw_bf[i][:], gwstg[:])
        gwc.append(G(_gg))

    def _v0(v):
        v.wait_ge(ldpre.h, pre_n)
        v.memset(ln["ones"][:], 1.0)
    v0 = V(_v0)

    def _t0(t):
        t.matmul(aux[:, 0:NH], ln["ones"][0:1, :], nrow_sb[0:1, :], start=True, stop=True)
    t0 = T_(_t0, waits=[(vs, v0)])

    def _va(v):
        v.tensor_copy(acol[:], aux[:, 0:NH])
    va = V(_va, waits=[(pe, t0)])

    st = {
        "q_used": None, "k_used": [None] * NKT, "v_used": [None, None],
        "sp_free": {0: (vs, va), 1: None, 2: None},
        "pb_free": [None, None],
        "aux_free": (vs, va),
        "att_used": None,
        "moe_mm_last": None,
        "y2r_used": [None, None],
        "acc_guard": None,
        "u1_used": [None, None],
        "gbv_used": None,
        "tmp_used": None,
    }
    wconv = [None, None]
    wuse = [None, None]

    def queue_wload(src_ap, slot):
        ldw[slot].n += 16
        need = ldw[slot].n

        def _yw(y, src_ap=src_ap, slot=slot, wc=wconv[slot], wu=wuse[slot]):
            if wc is not None:
                y.wait_ge(gp.h, wc)
            if wu is not None:
                y.wait_ge(pe.h, wu)
            y.dma_start(out=wstg[slot][:], in_=src_ap).then_inc(ldw[slot].h, 16)
        Y(_yw, mile=False)

        def _gw(g, slot=slot, need=need, wu=wuse[slot]):
            g.wait_ge(ldw[slot].h, need)
            if wu is not None:
                g.wait_ge(pe.h, wu)
            g.tensor_copy(wbf[slot][:], wstg[slot][:])
        wconv[slot] = G(_gw)
        return wconv[slot]

    def last_kc_mod(r, nkc):
        for k in range(nkc - 1, -1, -1):
            if k % 3 == r:
                return k
        return None

    weffoTr = d["weffoT_hbm"].rearrange("(c p) m -> p c m", p=P)

    def queue_bfload(src_ap, slot):
        ldw[slot].n += 16
        need = ldw[slot].n

        def _yw(y, src_ap=src_ap, slot=slot, wc=wconv[slot], wu=wuse[slot]):
            if wu is not None:
                y.wait_ge(pe.h, wu)
            if wc is not None:
                y.wait_ge(gp.h, wc)
            y.dma_start(out=wbf[slot][:], in_=src_ap).then_inc(ldw[slot].h, 16)
        Y(_yw, mile=False)
        return need

    kload_n = {}
    for qo in range(NQO):
        g_done = {}
        lnd_half = {}
        for qh in range(2):
            qt = qo * 2 + qh
            ldq.n += 8 * 16
            qn_ld = ldq.n

            def _yq(y, qt=qt, qu=st["q_used"]):
                if qu is not None:
                    y.wait_ge(pe.h, qu)
                for i in range(8):
                    y.dma_start(out=q_sb[i][:],
                                in_=d["q_hbm"][i * P:(i + 1) * P, qt * 512:(qt + 1) * 512]
                                ).then_inc(ldq.h, 16)
            Y(_yq, mile=False)

            exp_m = {}
            pv_m = {}
            att_evac = {}
            prev_exp = {}
            prev_pv = {}
            for h in range(NH):
                bs = (h % 2) * 64
                if h % 2 == 0:
                    for kq in range(NKQ):
                        ldk[kq % 4].n += 16
                        kload_n[kq] = ldk[kq % 4].n

                        def _yk(y, h=h, kq=kq, ku=st["k_used"][kq]):
                            if ku is not None:
                                y.wait_ge(pe.h, ku)
                            y.dma_start(out=ktile4[kq][:],
                                        in_=d["k_hbm"][(h // 2) * P:(h // 2 + 1) * P,
                                                       kq * 1024:(kq + 1) * 1024]
                                        ).then_inc(ldk[kq % 4].h, 16)
                        Y(_yk, mile=False)
                vsl = h % 2

                ldv[vsl].n += 16
                vn_ld = ldv[vsl].n

                def _yv(y, h=h, vsl=vsl, vu=st["v_used"][vsl]):
                    if vu is not None:
                        y.wait_ge(pe.h, vu)
                    y.dma_start(out=vh_t[vsl][:].rearrange("p (c f) -> p c f", f=65),
                                in_=d["vhat_hbm"][h].rearrange("(c p) f -> p c f", p=P)
                                ).then_inc(ldv[vsl].h, 16)
                Y(_yv, mile=False)

                for kc in range(NKC):
                    bnk = kc % 3

                    kq = kc // 8

                    tw = [(ldk[kq % 4], kload_n[kq])]
                    if h == 0 and kc == 0:
                        tw.append((ldq, qn_ld))
                    if exp_m.get((h, kc - 3)) is not None:
                        tw.append((ac, exp_m[(h, kc - 3)]))
                    elif prev_exp.get(bnk) is not None:
                        tw.append((ac, prev_exp[bnk]))
                    elif st["sp_free"][bnk] is not None:
                        tw.append(st["sp_free"][bnk])

                    def _ts(t, h=h, kc=kc, kq=kq, bs=bs, bnk=bnk):
                        t.matmul(sp[bnk][:], ktile4[kq][bs:bs + 64, (kc % 8) * P:(kc % 8 + 1) * P],
                                 q_sb[h // 2][bs:bs + 64, :], start=True, stop=True)
                    ts_m = T_(_ts, waits=tw)

                    sw = [(pe, ts_m)]
                    if pv_m.get((h, kc - 4)) is not None:
                        sw.append((pe, pv_m[(h, kc - 4)]))
                    elif prev_pv.get(kc % 4) is not None:
                        sw.append((pe, prev_pv[kc % 4]))
                    if h == 0 and kc == 0 and qt == 0:
                        sw.append((vs, va))

                    def _se(s, h=h, kc=kc, bnk=bnk):
                        s.activation(pp_bf[kc % 4][:], sp[bnk][:], AF.Exp, bias=acol[:, h:h + 1])
                    exp_m[(h, kc)] = S(_se, waits=sw)
                    prev_exp[bnk] = exp_m[(h, kc)]

                    pw = [(ldv[vsl], vn_ld), (ac, exp_m[(h, kc)])]
                    if kc == 0 and att_evac.get(h - 2) is not None:
                        pw.append((vs, att_evac[h - 2]))
                    if kc == 0 and h < 2 and st["pb_free"][h % 2] is not None:
                        pw.append(st["pb_free"][h % 2])

                    def _tp(t, h=h, kc=kc, vsl=vsl):
                        t.matmul(pb[h % 2][:], vh_t[vsl][:].rearrange("p (c f) -> p c f", f=65)[:, kc, :],
                                 pp_bf[kc % 4][:], start=(kc == 0), stop=(kc == NKC - 1))
                    pv_m[(h, kc)] = T_(_tp, waits=pw)
                    prev_pv[kc % 4] = pv_m[(h, kc)]
                if h % 2 == 1:
                    for kq in range(NKT):
                        st["k_used"][kq] = pv_m[(h, NKC - 1)]
                st["v_used"][vsl] = pv_m[(h, NKC - 1)]

                def _vr(v, h=h):
                    v.reciprocal(rr_sb[:], pb[h % 2][64:65, :])
                vr = V(_vr, waits=[(pe, pv_m[(h, NKC - 1)])])

                def _tb2(t, h=h):
                    t.matmul(aux[0:64, :], ln["ones"][0:1, 0:64], rr_sb[0:1, :], start=True, stop=True)
                tb2 = T_(_tb2, waits=[(vs, vr), st["aux_free"]])

                def _sb2(s, h=h):
                    s.copy(rb_sb[:], aux[0:64, :])
                sb2 = S(_sb2, waits=[(pe, tb2)])
                st["aux_free"] = (ac, sb2)

                w = [(ac, sb2)]
                if h == 0 and st["att_used"] is not None:
                    w.append((pe, st["att_used"]))

                def _vat(v, h=h, bs=bs):
                    v.tensor_tensor(att_bf[h // 2][bs:bs + 64, :], pb[h % 2][0:64, :],
                                    rb_sb[:], ALU.mult)
                att_evac[h] = V(_vat, waits=w)
                st["pb_free"][h % 2] = (vs, att_evac[h])
            st["q_used"] = pv_m[(NH - 1, NKC - 1)]

            y3_ready = {}
            op_bank = {0: None, 1: None}
            prev_vo = None
            for m in range(8):
                obank = sp[0] if m % 2 == 0 else sp[2]

                ldy2[m % 2].n += 16
                y2n = ldy2[m % 2].n

                def _yy2(y, m=m, qt=qt, yu=st["y2r_used"][m % 2]):
                    if yu is not None:
                        y.wait_ge(vs.h, yu)
                    y.dma_start(out=y2res[m % 2][:],
                                in_=d["y2f_hbm"][m * P:(m + 1) * P, qt * 512:(qt + 1) * 512]
                                ).then_inc(ldy2[m % 2].h, 16)
                Y(_yy2, mile=False)

                w = [(vs, att_evac[NH - 1])]
                if op_bank[m % 2] is not None:
                    w.append((vs, op_bank[m % 2]))
                else:
                    w.append((ac, exp_m[(NH - 1, NKC - 1)]))

                wo_n = queue_bfload(weffoTr[:, :, m * P:(m + 1) * P], m % 2)
                w.append((ldw[m % 2], wo_n))

                def _to(t, m=m, obank=obank):
                    for i in range(8):
                        t.matmul(obank[:], wbf[m % 2][:, i * P:(i + 1) * P], att_bf[i][:],
                                 start=(i == 0), stop=(i == 7))
                to = T_(_to, waits=w)
                wuse[m % 2] = to

                w = [(pe, to)]

                w.append((ldy2[m % 2], y2n))
                if m == 0 and qh == 0 and st["acc_guard"] is not None:
                    w.append(st["acc_guard"])

                def _vo(v, m=m, qh=qh, obank=obank):
                    v.tensor_scalar(y3scr[:], obank[:], bo_c[:, m:m + 1], None, ALU.add)
                    v.tensor_tensor(y3f[m][:, qh * 512:(qh + 1) * 512], y3scr[:],
                                    y2res[m % 2][:], ALU.add)
                vo = V(_vo, waits=w)
                op_bank[m % 2] = vo
                y3_ready[m] = vo
                prev_vo = vo
                st["y2r_used"][m % 2] = vo
            st["att_used"] = pe.n

            cw = [[(vs, y3_ready[c])] for c in range(8)]
            bw = [[(pe, st["moe_mm_last"])] if st["moe_mm_last"] else [] for c in range(8)]
            lnd = emit_ln(ctx, V, S, T_,
                          [y3f[c][:, qh * 512:(qh + 1) * 512] for c in range(8)],
                          [y3b[c][:, qh * 512:(qh + 1) * 512] for c in range(8)],
                          ln, chunk_waits=cw, ybf_waits=bw)
            lnd_half[qh] = lnd
            st["sp_free"] = {0: (vs, lnd[-1]), 1: (vs, lnd[-1]), 2: (vs, lnd[-1])}
            st["aux_free"] = (vs, lnd[-1])

            def _tg(t, qh=qh):
                for i in range(8):
                    t.matmul(aux[0:8, :], gw_bf[i][:], y3b[i][:, qh * 512:(qh + 1) * 512],
                             start=(i == 0), stop=(i == 7))
            tg = T_(_tg, waits=[(vs, lnd[-1]), (gp, gwc[-1]), st["aux_free"]])

            def _sg(v):
                v.tensor_scalar(lg_sb[:], aux[0:8, :], gb_sb[0:8, 0:1], None, ALU.add)
            sg = V(_sg, waits=[(pe, tg)])
            st["aux_free"] = (vs, sg)

            last_vg = None
            for tc in range(4):
                tw1 = [(vs, sg)]
                if tc == 0:
                    tw1.append(st["aux_free"])
                if last_vg is not None:
                    tw1.append((vs, last_vg))

                def _tt1(t, tc=tc):
                    t.transpose(aux[:, 0:8], lg_sb[0:8, tc * P:(tc + 1) * P], id_sb[0:8, 0:8])
                tt1 = T_(_tt1, waits=tw1)

                def _vg1(v, tt1=tt1):
                    v.wait_ge(pe.h, tt1)
                    v.tensor_copy(lt_sb[:], aux[:, 0:8])
                    v.reduce_max(m1_sb[:], lt_sb[:], axis=AX.X)
                    v.tensor_scalar_mul(nm1_sb[:], m1_sb[:], -1.0)
                    v.tensor_scalar(msk_sb[:], lt_sb[:], m1_sb[:, 0:1], None, ALU.is_ge)
                    v.scalar_tensor_tensor(msk_sb[:], msk_sb[:], -1e30, lt_sb[:], ALU.mult, ALU.add)
                    v.reduce_max(m2_sb[:], msk_sb[:], axis=AX.X)
                vg1 = V(_vg1)

                def _sg1(s, vg1=vg1):
                    s.wait_ge(vs.h, vg1)
                    s.activation(rden_sb[:], m2_sb[:], AF.Exp, bias=nm1_sb[:, 0:1])
                    s.activation(g_tm[:], lt_sb[:], AF.Exp, bias=nm1_sb[:, 0:1])
                sg1 = S(_sg1)

                def _vg2(v, sg1=sg1, tc=tc, qh=qh):
                    v.wait_ge(ac.h, sg1)
                    v.tensor_scalar_add(rden_sb[:], rden_sb[:], 1.0)
                    v.reciprocal(rden_sb[:], rden_sb[:])
                    v.tensor_scalar(msk_sb[:], lt_sb[:], m2_sb[:, 0:1], None, ALU.is_ge)
                    v.tensor_tensor(g_tm[:], g_tm[:], msk_sb[:], ALU.mult)
                    v.tensor_scalar_mul(g_tm[:], g_tm[:], rden_sb[:, 0:1])
                vg2 = V(_vg2)

                tw2 = [(vs, vg2)]
                if tc == 0 and st["pb_free"][0] is not None:
                    tw2.append(st["pb_free"][0])

                def _tt2(t, tc=tc, qh=qh):
                    t.transpose(pb[0][0:8, tc * P:(tc + 1) * P], g_tm[:], id_sb[:])
                tt2 = T_(_tt2, waits=tw2)

                vw4 = [(pe, tt2)]
                if qh == 0 and tc == 0:
                    vw4.append((gfq, gfq.n))

                def _vg4(v, tc=tc, qh=qh):
                    v.tensor_copy(g_sb[0:8, qh * 512 + tc * P: qh * 512 + (tc + 1) * P],
                                  pb[0][0:8, tc * P:(tc + 1) * P])
                last_vg = V(_vg4, waits=vw4)
            g_done[qh] = last_vg
            st["pb_free"][0] = (vs, last_vg)

        w = [(vs, g_done[1])]
        if st["moe_mm_last"] is not None:
            w.append((pe, st["moe_mm_last"]))

        def _vg5(v):
            v.tensor_copy(g_bf[:], g_sb[:])
        vg5 = V(_vg5, waits=w)

        gfq.n += 32
        gfl_n = gfq.n

        def _gfl(g, mml=st["moe_mm_last"]):
            g.wait_ge(vs.h, vg5)
            if mml is not None:
                g.wait_ge(pe.h, mml)
            g.dma_start(out=gflat[0:1, :], in_=g_sb[0:4, :]).then_inc(gfq.h, 16)
            g.dma_start(out=gflat[64:65, :], in_=g_sb[4:8, :]).then_inc(gfq.h, 16)
        G(_gfl, mile=False)

        acc_init = {}
        prev_vb3 = None
        for qh2 in range(2):
            for dc in range(8):
                w = [(vs, vg5)]
                if qh2 == 0 and dc == 0:
                    w.append(st["sp_free"][2] or (vs, None))
                    if g0 is not None:
                        w.append((gp, g0))
                if prev_vb3 is not None:
                    w.append((vs, prev_vb3))

                def _tb3(t, qh2=qh2, dc=dc):
                    t.matmul(sp[2][:], b2_sb[0:8, dc * P:(dc + 1) * P],
                             g_bf[0:8, qh2 * 512:(qh2 + 1) * 512], start=True, stop=True)
                tb3 = T_(_tb3, waits=w)

                w = [(pe, tb3)]
                if st["acc_guard"] is not None:
                    w.append(st["acc_guard"])

                def _vb3(v, qh2=qh2, dc=dc):
                    v.tensor_tensor(y3f[dc][:, qh2 * 512:(qh2 + 1) * 512], sp[2][:],
                                    y3f[dc][:, qh2 * 512:(qh2 + 1) * 512], ALU.add)
                prev_vb3 = V(_vb3, waits=w)
                acc_init[(qh2, dc)] = prev_vb3
        st["sp_free"][2] = (vs, prev_vb3)

        acc_last = {}
        for ee in range(NE):
            h0_free = st["sp_free"][0]
            h1_free = st["sp_free"][1]
            act_done = {}
            for m in range(16):
                c0 = queue_wload(d["w0mTr"][ee][:, :, m * P:(m + 1) * P], 0)
                c1 = queue_wload(d["w1mTr"][ee][:, :, m * P:(m + 1) * P], 1)
                for qh2 in range(2):
                    w = [(gp, c0)]
                    if ee == 0 and m == 0 and qh2 == 0:
                        w.append((vs, lnd_half[1][-1]))
                    if h0_free is not None:
                        w.append(h0_free)

                    def _t0m(t, m=m, qh2=qh2):
                        for i in range(8):
                            t.matmul(sp[0][:], wbf[0][:, i * P:(i + 1) * P],
                                     y3b[i][:, qh2 * 512:(qh2 + 1) * 512],
                                     start=(i == 0), stop=(i == 7))
                    t0m = T_(_t0m, waits=w)
                    wuse[0] = t0m

                    w = [(gp, c1)]
                    if h1_free is not None:
                        w.append(h1_free)

                    def _t1m(t, m=m, qh2=qh2):
                        for i in range(8):
                            t.matmul(sp[1][:], wbf[1][:, i * P:(i + 1) * P],
                                     y3b[i][:, qh2 * 512:(qh2 + 1) * 512],
                                     start=(i == 0), stop=(i == 7))
                    t1m = T_(_t1m, waits=w)
                    wuse[1] = t1m

                    w = [(pe, t1m)]
                    if st["u1_used"][qh2] is not None:
                        w.append((vs, st["u1_used"][qh2]))

                    def _s1m(s, ee=ee, m=m, qh2=qh2):
                        s.activation(u1scr[qh2][:], sp[1][:], AF.Silu, bias=b1c_sb[ee][:, m:m + 1])
                    s1m = S(_s1m, waits=w)
                    h1_free = (ac, s1m)

                    def _v1m(v, ee=ee, m=m, qh2=qh2, s1m=s1m):
                        v.wait_ge(ac.h, s1m)
                        dst = act_e[m // 2][:, (m % 2) * 1024 + qh2 * 512:(m % 2) * 1024 + (qh2 + 1) * 512]
                        v.scalar_tensor_tensor(dst, sp[0][:], b0c_sb[ee][:, m:m + 1],
                                               u1scr[qh2][:], ALU.add, ALU.mult)
                    v1m = V(_v1m, waits=[(pe, t0m)])
                    h0_free = (vs, v1m)
                    st["u1_used"][qh2] = v1m
                    act_done[(m, qh2)] = v1m
            st["sp_free"][0] = h0_free
            st["sp_free"][1] = h1_free

            for qh2 in range(2):
                w = [(gfq, gfl_n), st["aux_free"]]

                def _tgb(t, ee=ee, qh2=qh2):
                    idx = ee * 1024 + qh2 * 512
                    if idx < 4096:
                        row, one = gflat[0:1, idx:idx + 512], ln["ones"][0:1, :]
                    else:
                        row, one = gflat[64:65, idx - 4096:idx - 4096 + 512], ln["ones"][64:65, :]
                    t.matmul(aux[:], one, row, start=True, stop=True)
                tgb = T_(_tgb, waits=w)

                w = [(pe, tgb)]
                if st["gbv_used"] is not None:
                    w.append((vs, st["gbv_used"]))

                def _vgb(v):
                    v.tensor_copy(gbv_sb[:], aux[:])
                vgb = V(_vgb, waits=w)
                st["aux_free"] = (vs, vgb)

                for dc in range(8):
                    half = dc % 2
                    c2 = queue_wload(d["w2mTr"][ee][:, 0:8, dc * P:(dc + 1) * P], 0)
                    c3 = queue_wload(d["w2mTr"][ee][:, 8:16, dc * P:(dc + 1) * P], 1)
                    w = [(gp, c2), (ee == 0 and qh2 == 0 and dc == 0) and st["sp_free"][2] or (vs, None)]
                    w = [(gp, c2)]
                    if st["sp_free"][2] is not None:
                        w.append(st["sp_free"][2])

                    def _t2a(t, qh2=qh2):
                        for i in range(8):
                            t.matmul(sp[2][:], wbf[0][:, i * P:(i + 1) * P],
                                     act_e[i // 2][:, (i % 2) * 1024 + qh2 * 512:(i % 2) * 1024 + (qh2 + 1) * 512],
                                     start=(i == 0), stop=False)
                    t2a = T_(_t2a, waits=w + [(vs, act_done[(15, qh2)])])
                    wuse[0] = t2a

                    def _t2b(t, qh2=qh2):
                        for i in range(8, 16):
                            t.matmul(sp[2][:], wbf[1][:, (i - 8) * P:(i - 7) * P],
                                     act_e[i // 2][:, (i % 2) * 1024 + qh2 * 512:(i % 2) * 1024 + (qh2 + 1) * 512],
                                     start=False, stop=(i == 15))
                    t2b = T_(_t2b, waits=[(gp, c3)])
                    wuse[1] = t2b

                    w = [(pe, t2b), (vs, vgb)]
                    if st["tmp_used"] is not None:
                        w.append((vs, st["tmp_used"]))

                    def _v2a(v):
                        v.tensor_tensor(tmp_f[:], sp[2][:], gbv_sb[:], ALU.mult)
                    v2a = V(_v2a, waits=w)
                    st["sp_free"][2] = (vs, v2a)

                    wprev = acc_init[(qh2, dc)] if ee == 0 else acc_last[(qh2, dc)]

                    def _v2b(v, qh2=qh2, dc=dc):
                        v.tensor_tensor(y3f[dc][:, qh2 * 512:(qh2 + 1) * 512],
                                        y3f[dc][:, qh2 * 512:(qh2 + 1) * 512], tmp_f[:], ALU.add)
                    v2b = V(_v2b, waits=[(vs, wprev)])
                    acc_last[(qh2, dc)] = v2b
                    st["tmp_used"] = v2b
                st["gbv_used"] = v2b
        st["moe_mm_last"] = pe.n

        store_n0 = stout.n
        for qh2 in range(2):
            cw = [[(vs, acc_last[(qh2, c)])] for c in range(8)]
            lnf = emit_ln(ctx, V, S, T_,
                          [y3f[c][:, qh2 * 512:(qh2 + 1) * 512] for c in range(8)],
                          None, ln, chunk_waits=cw)
            st["sp_free"][2] = (vs, lnf[-1])
            st["aux_free"] = (vs, lnf[-1])
            for c in range(8):
                stout.n += 16

                def _gso(g, c=c, qh2=qh2, qo=qo, lnf=lnf):
                    g.wait_ge(vs.h, lnf[c])
                    g.dma_start(out=d["outT"][c * P:(c + 1) * P,
                                             qo * 1024 + qh2 * 512: qo * 1024 + (qh2 + 1) * 512],
                                in_=y3f[c][:, qh2 * 512:(qh2 + 1) * 512]).then_inc(stout.h, 16)
                G(_gso, mile=False)
        st["acc_guard"] = (stout, stout.n)

    def _gfin(g, nf=stout.n):
        g.wait_ge(stout.h, nf)
    G(_gfin, mile=False)
    ctx.flush()



def _col_layout(v, rows=P):
    n = v.shape[0]
    return np.ascontiguousarray(v.reshape(n // rows, rows).T)


def prep_inputs(inputs):
    t = {}
    t["ident"] = np.eye(P, dtype=np.float32)
    t["wqkvT"] = np.ascontiguousarray(inputs["L0_attn_W_qkv"].T)
    t["w1T"] = np.ascontiguousarray(inputs["L0_ffn_w1_w"].T)
    t["w2T"] = np.ascontiguousarray(inputs["L0_ffn_w2_w"].T)
    t["h1T"] = np.ascontiguousarray(inputs["L0_ffn_h1_w"].T)
    t["h2T"] = np.ascontiguousarray(inputs["L0_ffn_h2_w"].T)
    t["w3T"] = np.ascontiguousarray(inputs["L0_ffn_w3_w"].T)
    t["s1c"] = _col_layout(inputs["L0_ffn_w1_s"])
    t["s2c"] = _col_layout(inputs["L0_ffn_w2_s"])
    t["sh1c"] = _col_layout(inputs["L0_ffn_h1_s"])
    t["sh2c"] = _col_layout(inputs["L0_ffn_h2_s"])
    t["s3c"] = _col_layout(inputs["L0_ffn_w3_s"])
    for p, Pn in [("q", "q"), ("k", "k"), ("v", "v"), ("o", "o")]:
        Pu = Pn.capitalize() if False else Pn
        t["w_" + p] = np.ascontiguousarray(inputs[f"L1_attn_W{Pn}"].T)
        t["a_" + p] = np.ascontiguousarray(inputs[f"L1_attn_A{Pn}"])
        t["bt_" + p] = np.ascontiguousarray(inputs[f"L1_attn_B{Pn}"].T)
    t["bq_c"] = _col_layout(inputs["L1_attn_bq"])
    t["bk_c"] = _col_layout(inputs["L1_attn_bk"])
    t["bv_row"] = np.ascontiguousarray(inputs["L1_attn_bv"].reshape(1, D))
    t["bo_c"] = _col_layout(inputs["L1_attn_bo"])
    t["gwT"] = np.ascontiguousarray(inputs["L1_ffn_gate_w"].T)
    t["gb_c"] = np.ascontiguousarray(inputs["L1_ffn_gate_b"].reshape(NE, 1))
    t["w0mT"] = np.ascontiguousarray(inputs["L1_ffn_w0"].transpose(0, 2, 1))
    t["w1mT"] = np.ascontiguousarray(inputs["L1_ffn_w1"].transpose(0, 2, 1))
    t["w2mT"] = np.ascontiguousarray(inputs["L1_ffn_w2"].transpose(0, 2, 1))
    t["b0c"] = np.ascontiguousarray(inputs["L1_ffn_b0"].reshape(NE, 16, P).transpose(0, 2, 1))
    t["b1c"] = np.ascontiguousarray(inputs["L1_ffn_b1"].reshape(NE, 16, P).transpose(0, 2, 1))
    t["b2m"] = np.ascontiguousarray(inputs["L1_ffn_b2"])
    return t


def declare_stage1(nc, T, kinds=("ExternalInput", "ExternalOutput")):
    ki, ko = kinds
    d = {}
    d["xT"] = nc.dram_tensor("xT", [D, T], F32, kind=ki).ap()
    d["wqkvT"] = nc.dram_tensor("wqkvT", [D, 3072], F32, kind=ki).ap()
    d["ident"] = nc.dram_tensor("ident", [P, P], F32, kind=ki).ap()
    d["qf_hbm"] = nc.dram_tensor("qf_hbm", [T, D], BF16, kind=ko).ap()
    d["kv2_hbm"] = nc.dram_tensor("kv2_hbm", [P, 2048], F32, kind=ko).ap()
    d["ksrow_hbm"] = nc.dram_tensor("ksrow_hbm", [1, D], F32, kind=ko).ap()
    return d


def declare_stage2(nc, T, d=None, kinds=("ExternalInput", "ExternalOutput")):
    ki, ko = kinds
    if d is None:
        d = {}
    def di(name, shape, dt, kind):
        if name not in d:
            d[name] = nc.dram_tensor(name, shape, dt, kind=kind).ap()
    di("xT", [D, T], F32, ki)
    di("qf_hbm", [T, D], BF16, ki)
    di("kv2_hbm", [P, 2048], F32, ki)
    di("ksrow_hbm", [1, D], F32, ki)
    di("s1c", [P, 32], F32, ki)
    di("s2c", [P, 32], F32, ki)
    di("sh1c", [P, 32], F32, ki)
    di("sh2c", [P, 32], F32, ki)
    di("s3c", [P, 8], F32, ki)
    for nm, shp in [("w1T", [D, DH]), ("w2T", [D, DH]), ("h1T", [DH, DH]),
                    ("h2T", [DH, DH]), ("w3T", [DH, D])]:
        if nm not in d:
            d[nm] = nc.dram_tensor(nm, shp, F32, kind=ki).ap()
        d[nm + "r"] = d[nm].rearrange("(c p) m -> p c m", p=P)
    di("qfs_hbm", [T, D], BF16, "Internal")
    di("y2f_hbm", [D, T], F32, ko)
    di("y2b_hbm", [D, T], BF16, ko)
    return d


def declare_stage3(nc, T, d=None, kinds=("ExternalInput", "ExternalOutput")):
    ki, ko = kinds
    if d is None:
        d = {}

    def di(name, shape, dt, kind):
        if name not in d:
            d[name] = nc.dram_tensor(name, shape, dt, kind=kind).ap()
    di("y2b_hbm", [D, T], BF16, ki)
    di("ident", [P, P], F32, ki)
    for p in ["q", "k", "v", "o"]:
        di("w_" + p, [D, D], F32, ki)
        di("a_" + p, [8, D], F32, ki)
        di("bt_" + p, [8, D], F32, ki)
    di("bq_c", [P, 8], F32, ki)
    di("bk_c", [P, 8], F32, ki)
    di("bv_row", [1, D], F32, ki)
    di("q_hbm", [D, T], BF16, ko)
    di("k_hbm", [D, T], BF16, ko)
    di("vhat_hbm", [NH, T, 65], BF16, ko)
    di("weffoT_hbm", [D, D], BF16, ko)
    di("negarow_hbm", [1, NH], F32, ko)
    return d


def declare_stage4(nc, T, d=None, kinds=("ExternalInput", "ExternalOutput")):
    ki, ko = kinds
    if d is None:
        d = {}

    def di(name, shape, dt, kind):
        if name not in d:
            d[name] = nc.dram_tensor(name, shape, dt, kind=kind).ap()
    di("q_hbm", [D, T], BF16, ki)
    di("k_hbm", [D, T], BF16, ki)
    di("vhat_hbm", [NH, T, 65], BF16, ki)
    di("weffoT_hbm", [D, D], BF16, ki)
    di("negarow_hbm", [1, NH], F32, ki)
    di("y2f_hbm", [D, T], F32, ki)
    di("ident", [P, P], F32, ki)
    di("bo_c", [P, 8], F32, ki)
    di("gwT", [D, 8], F32, ki)
    di("gb_c", [8, 1], F32, ki)
    di("w0mT", [NE, D, MH], F32, ki)
    di("w1mT", [NE, D, MH], F32, ki)
    di("w2mT", [NE, MH, D], F32, ki)
    di("b0c", [NE, P, 16], F32, ki)
    di("b1c", [NE, P, 16], F32, ki)
    di("b2m", [NE, D], F32, ki)
    di("outT", [D, T], F32, ko)
    for nm in ["w0mT", "w1mT", "w2mT"]:
        d[nm + "r"] = [d[nm][ee].rearrange("(c p) m -> p c m", p=P) for ee in range(NE)]
    return d


S1_KEYS = ["xT", "wqkvT", "ident"]
S2_KEYS = ["xT", "qf_hbm", "kv2_hbm", "ksrow_hbm", "s1c", "s2c", "sh1c", "sh2c", "s3c",
           "w1T", "w2T", "h1T", "h2T", "w3T"]
S3_KEYS = ["y2b_hbm", "ident", "w_q", "a_q", "bt_q", "w_k", "a_k", "bt_k",
           "w_v", "a_v", "bt_v", "w_o", "a_o", "bt_o", "bq_c", "bk_c", "bv_row"]
S4_KEYS = ["q_hbm", "k_hbm", "vhat_hbm", "weffoT_hbm", "negarow_hbm", "y2f_hbm",
           "ident", "bo_c", "gwT", "gb_c", "w0mT", "w1mT", "w2mT", "b0c", "b1c", "b2m"]


def _build(stage, T):
    nc = bass.Bass("TRN2", target_bir_lowering=False)
    ex = ExitStack()
    declare, emit = {
        1: (declare_stage1, emit_stage1),
        2: (declare_stage2, emit_stage2),
        3: (declare_stage3, emit_stage3),
        4: (declare_stage4, emit_stage4),
    }[stage]
    d = declare(nc, T)
    with ExitStack() as ex:
        ctx = Ctx(nc, ex, f"s{stage}")
        emit(ctx, d, T)
    return nc


def build_fused(T):
    nc = bass.Bass("TRN2", target_bir_lowering=False)
    d = {}
    declare_stage1(nc, T, kinds=("ExternalInput", "Internal"))
    return None


def _declare_all(nc, T):
    d = {}

    def di(name, shape, dt, kind):
        if name not in d:
            d[name] = nc.dram_tensor(name, shape, dt, kind=kind).ap()
    di("xT", [D, T], F32, "ExternalInput")
    di("ident", [P, P], F32, "ExternalInput")
    di("wqkvT", [D, 3072], F32, "ExternalInput")
    for nm, shape, dt in [("qf_hbm", [T, D], BF16), ("kv2_hbm", [P, 2048], F32),
                          ("ksrow_hbm", [1, D], F32), ("qfs_hbm", [T, D], BF16),
                          ("y2f_hbm", [D, T], F32), ("y2b_hbm", [D, T], BF16),
                          ("q_hbm", [D, T], BF16), ("k_hbm", [D, T], BF16),
                          ("vhat_hbm", [NH, T, 65], BF16), ("weffoT_hbm", [D, D], BF16),
                          ("negarow_hbm", [1, NH], F32)]:
        di(nm, shape, dt, "Internal")
    declare_stage2(nc, T, d=d)
    declare_stage3(nc, T, d=d)
    declare_stage4(nc, T, d=d)
    d["outT"] = d.get("outT") or nc.dram_tensor("outT", [D, T], F32, kind="ExternalOutput").ap()
    return d


def build_fused_nc(T):
    nc = bass.Bass("TRN2", target_bir_lowering=False)
    master = ExitStack()
    with master:
        d = _declare_all(nc, T)
        done = Sem(master.enter_context(nc.semaphore("stage_done")))
        for stage, emit in [(1, emit_stage1), (2, emit_stage2),
                            (3, emit_stage3), (4, emit_stage4)]:
            with ExitStack() as ex:
                ctx = Ctx(nc, ex, f"s{stage}", sem_ex=master)
                if stage > 1:
                    ctx.barrier_in(done, done.n)
                emit(ctx, d, T)
                ctx.barrier_out(done)
    return nc


ALL_IN_KEYS = ["xT", "ident", "wqkvT", "s1c", "s2c", "sh1c", "sh2c", "s3c",
               "w1T", "w2T", "h1T", "h2T", "w3T",
               "w_q", "a_q", "bt_q", "w_k", "a_k", "bt_k",
               "w_v", "a_v", "bt_v", "w_o", "a_o", "bt_o",
               "bq_c", "bk_c", "bv_row", "bo_c", "gwT", "gb_c",
               "w0mT", "w1mT", "w2mT", "b0c", "b1c", "b2m"]


def kernel(**inputs):
    x = np.asarray(inputs["L0_attn_x"])
    B, T, _ = x.shape
    sh = prep_inputs(inputs)
    nc = build_fused_nc(T)
    maps = []
    for b in range(B):
        m = {k: sh[k] for k in ALL_IN_KEYS if k != "xT"}
        m["xT"] = np.ascontiguousarray(x[b].T)
        maps.append(m)
    res = run_bass_kernel_spmd(nc, maps, core_ids=list(range(B))).results
    out = np.stack([np.asarray(res[b]["outT"]).T for b in range(B)], axis=0)
    return np.ascontiguousarray(out.astype(np.float32))


def kernel_multilaunch(**inputs):
    x = np.asarray(inputs["L0_attn_x"])
    B, T, _ = x.shape
    sh = prep_inputs(inputs)
    xT = [np.ascontiguousarray(x[b].T) for b in range(B)]
    cores = list(range(B))

    def run(stage, per_core, keys):
        nc = _build(stage, T)
        maps = []
        for b in range(B):
            m = {}
            for k in keys:
                m[k] = per_core[b][k] if k in per_core[b] else sh[k]
            maps.append(m)
        return run_bass_kernel_spmd(nc, maps, core_ids=cores).results

    pc = [{"xT": xT[b]} for b in range(B)]
    r1 = run(1, pc, S1_KEYS)
    for b in range(B):
        pc[b].update({k: r1[b][k] for k in ["qf_hbm", "kv2_hbm", "ksrow_hbm"]})
    r2 = run(2, pc, S2_KEYS)
    for b in range(B):
        pc[b].update({k: r2[b][k] for k in ["y2f_hbm", "y2b_hbm"]})
    r3 = run(3, pc, S3_KEYS)
    for b in range(B):
        pc[b].update({k: r3[b][k] for k in
                      ["q_hbm", "k_hbm", "vhat_hbm", "weffoT_hbm", "negarow_hbm"]})
    r4 = run(4, pc, S4_KEYS)
    out = np.stack([np.asarray(r4[b]["outT"]).T for b in range(B)], axis=0)
    return np.ascontiguousarray(out.astype(np.float32))


# version 08 (final)
# speedup vs baseline: 1.1575x; 1.1575x over previous
import numpy as np
from contextlib import ExitStack

import concourse.bass as bass
import concourse.mybir as mybir
from concourse.bass_utils import run_bass_kernel_spmd

F32 = mybir.dt.float32
BF16 = mybir.dt.bfloat16
AF = mybir.ActivationFunctionType
ALU = mybir.AluOpType
AX = mybir.AxisListType

P = 128
D = 1024
NH = 16
HD = 64
DH = 4096
NE = 8
MH = 2048
EPS_LN = 1e-5
EPS_LA = 1e-6
L2_EPS2 = 1e-24
ATT_SCALE = 0.125


class Sem:
    def __init__(self, h):
        self.h = h
        self.n = 0


class Ctx:

    def __init__(self, nc, ex, tag, sem_ex=None):
        self.nc = nc
        self.ex = ex
        self.sem_ex = sem_ex if sem_ex is not None else ex
        self.tag = tag
        self.blk = ex.enter_context(nc.Block())
        self.dirty = {"v": False, "s": False, "g": False}
        self.pend = {k: [] for k in ["v", "s", "g", "t", "y"]}
        self.waited = {k: {} for k in ["v", "s", "g", "t", "y"]}
        self.sem = {}
        for name in ["ld", "pe", "vs", "ac", "gp", "stq"]:
            self.sem[name] = Sem(self.sem_ex.enter_context(nc.semaphore(f"{tag}_{name}")))

    def new_sem(self, name):
        return Sem(self.sem_ex.enter_context(self.nc.semaphore(f"{self.tag}_{name}")))

    def barrier_out(self, done):
        self.flush()
        for section, eng, drain in [(self.blk.vector, self.nc.vector, True),
                                    (self.blk.scalar, self.nc.scalar, True),
                                    (self.blk.gpsimd, self.nc.gpsimd, True),
                                    (self.blk.tensor, self.nc.tensor, False),
                                    (self.blk.sync, self.nc.sync, False)]:
            def f(_e, eng=eng, drain=drain):
                ins = eng.drain() if drain else eng.nop()
                ins.then_inc(done.h, 1)
                done.n += 1
            section(f)

    def barrier_in(self, done, val):
        self.flush()
        for section in [self.blk.vector, self.blk.scalar, self.blk.gpsimd,
                        self.blk.tensor, self.blk.sync]:
            def f(_e, val=val):
                _e.wait_ge(done.h, val)
            section(f)

    class _Pre:
        _raw = {"wait_ge", "dma_start", "nop", "drain", "indirect_dma_start"}

        def __init__(self, ctx, key, eng):
            object.__setattr__(self, "_c", (ctx, key, eng))

        def __getattr__(self, name):
            ctx, key, eng = object.__getattribute__(self, "_c")
            fn = getattr(eng, name)
            if name in Ctx._Pre._raw:
                return fn

            def wrap(*a, **k):
                if ctx.dirty[key]:
                    eng.drain()
                ctx.dirty[key] = True
                return fn(*a, **k)
            return wrap

    class _Rec:

        def __init__(self, eng):
            object.__setattr__(self, "_e", eng)
            object.__setattr__(self, "last", None)

        def __getattr__(self, name):
            eng = object.__getattribute__(self, "_e")
            fn = getattr(eng, name)
            if name in ("wait_ge",):
                return fn

            def wrap(*a, **k):
                r = fn(*a, **k)
                try:
                    object.__setattr__(self, "last", r)
                except Exception:
                    pass
                return r
            return wrap

    def _mk(self, section, eng, key, mile_sem):
        is_da = key in ("v", "s", "g")

        def run(body, waits=(), mile=True):
            seen = self.waited[key]
            todo = []
            for (sem, val) in waits:
                if val is None:
                    continue
                kid = id(sem)
                if seen.get(kid, -1) < val:
                    seen[kid] = val
                    todo.append((sem.h, val))
            mval = None
            if mile:
                mile_sem.n += 1
                mval = mile_sem.n

            def emit():
                _e = eng
                for (h, val) in todo:
                    _e.wait_ge(h, val)
                if is_da:
                    if key == "g":
                        rec = Ctx._Rec(eng)
                        body(rec)
                        if mile:
                            last = object.__getattribute__(rec, "last")
                            ins = last if last is not None else eng.drain()
                            ins.then_inc(mile_sem.h, 1)
                            self.dirty[key] = True
                    else:
                        body(Ctx._Pre(self, key, eng))
                        if mile:
                            dr = eng.drain()
                            self.dirty[key] = False
                            dr.then_inc(mile_sem.h, 1)
                else:
                    rec = Ctx._Rec(eng)
                    body(rec)
                    if mile:
                        last = object.__getattribute__(rec, "last")
                        ins = last if last is not None else eng.nop()
                        ins.then_inc(mile_sem.h, 1)
            self.pend[key].append(emit)
            if len(self.pend[key]) >= 64:
                self._flush_key(key)
            return mval
        return run

    def _flush_key(self, key):
        closures = self.pend[key]
        if not closures:
            return
        self.pend[key] = []
        section = {"v": self.blk.vector, "s": self.blk.scalar, "g": self.blk.gpsimd,
                   "t": self.blk.tensor, "y": self.blk.sync}[key]

        def f(_e):
            for c in closures:
                c()
        section(f)

    def flush(self):
        for key in ["y", "g", "t", "s", "v"]:
            self._flush_key(key)

    def make_steps(self):
        nc, blk = self.nc, self.blk
        V = self._mk(blk.vector, nc.vector, "v", self.sem["vs"])
        S = self._mk(blk.scalar, nc.scalar, "s", self.sem["ac"])
        G = self._mk(blk.gpsimd, nc.gpsimd, "g", self.sem["gp"])
        T_ = self._mk(blk.tensor, nc.tensor, "t", self.sem["pe"])
        Y = self._mk(blk.sync, nc.sync, "y", self.sem["ld"])
        return V, S, G, T_, Y


def _stream_loads(ctx, Y, G, triples, tag):
    gp = ctx.sem["gp"]
    lds = [ctx.new_sem(f"{tag}l{i}") for i in range(2)]
    miles = []
    for k, (src, stg, dst) in enumerate(triples):
        sl = lds[k % 2]
        sl.n += 16
        need = sl.n

        def _y(y, k=k, src=src, stg=stg, sl=sl):
            if k >= 2:
                y.wait_ge(gp.h, miles[k - 2])
            y.dma_start(out=stg, in_=src).then_inc(sl.h, 16)
        Y(_y, mile=False)

        def _g(g, stg=stg, dst=dst, need=need, sl=sl):
            g.wait_ge(sl.h, need)
            g.tensor_copy(dst, stg)
        miles.append(G(_g))
    return miles



def make_ln_scratch(nc, e, tag):
    ln = {}
    ln["sq"] = [e(nc.sbuf_tensor(f"{tag}sq{i}", [P, 512], F32)) for i in range(2)]
    ln["ones"] = e(nc.sbuf_tensor(f"{tag}ones", [P, P], F32))
    ln["m_sb"] = e(nc.sbuf_tensor(f"{tag}m", [1, 512], F32))
    ln["e2_sb"] = e(nc.sbuf_tensor(f"{tag}e2", [1, 512], F32))
    ln["var_sb"] = e(nc.sbuf_tensor(f"{tag}var", [1, 512], F32))
    ln["rstd_sb"] = e(nc.sbuf_tensor(f"{tag}rstd", [1, 512], F32))
    ln["mrstd_sb"] = e(nc.sbuf_tensor(f"{tag}mrstd", [1, 512], F32))
    ln["st_s"] = e(nc.psum_tensor(f"{tag}sts", [1, 512], F32))
    ln["st_q"] = e(nc.psum_tensor(f"{tag}stq", [1, 512], F32))
    ln["bc_r"] = e(nc.psum_tensor(f"{tag}bcr", [P, 512], F32))
    ln["bc_m"] = e(nc.psum_tensor(f"{tag}bcm", [P, 512], F32))
    return ln


def emit_ln(ctx, V, S, T_, r_chunks, ybf_chunks, ln, chunk_waits=None, ybf_waits=None):
    pe, vs, ac = ctx.sem["pe"], ctx.sem["vs"], ctx.sem["ac"]
    n = len(r_chunks)
    mm_m = [None, None]
    for c in range(n):
        w = list(chunk_waits[c]) if chunk_waits else []
        if mm_m[c % 2] is not None:
            w.append((pe, mm_m[c % 2]))
        if c < 2 and ln.get("_last_apply") is not None:
            w.append((vs, ln["_last_apply"]))

        def _s(s, c=c):
            s.activation(ln["sq"][c % 2][:], r_chunks[c], AF.Square)
        sqm = S(_s, waits=w)

        def _t(t, c=c, sqm=sqm):
            t.wait_ge(ac.h, sqm)
            t.matmul(ln["st_s"][:], ln["ones"][:, 0:1], r_chunks[c], start=(c == 0), stop=(c == n - 1))
            t.matmul(ln["st_q"][:], ln["ones"][:, 0:1], ln["sq"][c % 2][:], start=(c == 0), stop=(c == n - 1))
        mm_m[c % 2] = T_(_t)
    stats_done = mm_m[(n - 1) % 2]

    def _v1(v):
        v.tensor_scalar_mul(ln["m_sb"][:], ln["st_s"][:], 1.0 / D)
        v.tensor_scalar_mul(ln["e2_sb"][:], ln["st_q"][:], 1.0 / D)
        v.tensor_tensor(ln["var_sb"][:], ln["m_sb"][:], ln["m_sb"][:], ALU.mult)
        v.tensor_tensor(ln["var_sb"][:], ln["e2_sb"][:], ln["var_sb"][:], ALU.subtract)
        v.tensor_scalar_add(ln["var_sb"][:], ln["var_sb"][:], EPS_LN)
    v1 = V(_v1, waits=[(pe, stats_done)])

    def _s2(s):
        s.activation(ln["rstd_sb"][:], ln["var_sb"][:], AF.Ln)
        s.activation(ln["rstd_sb"][:], ln["rstd_sb"][:], AF.Exp, scale=-0.5)
    s2 = S(_s2, waits=[(vs, v1)])

    def _v2(v):
        v.tensor_tensor(ln["mrstd_sb"][:], ln["m_sb"][:], ln["rstd_sb"][:], ALU.mult)
    v2 = V(_v2, waits=[(ac, s2)])

    def _t2(t):
        t.matmul(ln["bc_r"][:], ln["ones"][0:1, :], ln["rstd_sb"][:], start=True, stop=True)
        t.matmul(ln["bc_m"][:], ln["ones"][0:1, :], ln["mrstd_sb"][:], start=True, stop=True)
    t2 = T_(_t2, waits=[(vs, v2)])

    out_m = []
    for c in range(n):
        w = [(pe, t2)]
        if ybf_waits:
            w += list(ybf_waits[c])

        def _v3(v, c=c):
            v.tensor_tensor(ln["sq"][c % 2][:], r_chunks[c], ln["bc_r"][:], ALU.mult)
            v.tensor_tensor(r_chunks[c], ln["sq"][c % 2][:], ln["bc_m"][:], ALU.subtract)
            if ybf_chunks is not None:
                v.tensor_copy(ybf_chunks[c], r_chunks[c])
        out_m.append(V(_v3, waits=w))
    ln["_last_apply"] = out_m[-1]
    return out_m



def emit_stage1(ctx, d, T):
    nc, e = ctx.nc, ctx.ex.enter_context
    NT = T // P
    xstage = [e(nc.sbuf_tensor(f"s1xs{i}", [P, T], F32)) for i in range(2)]
    x_bf = [e(nc.sbuf_tensor(f"s1xb{i}", [P, T], BF16)) for i in range(8)]
    wstage = [e(nc.sbuf_tensor(f"s1ws{i}", [P, 3072], F32)) for i in range(2)]
    w_bf = [e(nc.sbuf_tensor(f"s1wb{i}", [P, 3072], BF16)) for i in range(8)]
    sqf = e(nc.sbuf_tensor("s1sqf", [P, 512], F32))
    sq16 = e(nc.sbuf_tensor("s1sq16", [P, 8], F32))
    rn16 = e(nc.sbuf_tensor("s1rn16", [P, 8], F32))
    qn = e(nc.sbuf_tensor("s1qn", [P, 512], F32))
    esb = e(nc.sbuf_tensor("s1e", [P, 512], F32))
    qf_sb = [e(nc.sbuf_tensor(f"s1qf{i}", [P, D], BF16)) for i in range(2)]
    kf_sb = e(nc.sbuf_tensor("s1kf", [P, D], BF16))
    vh_sb = e(nc.sbuf_tensor("s1vh", [P, NH * 65], BF16))
    kv2 = e(nc.sbuf_tensor("s1kv2", [P, 2048], F32))
    kv_acc = e(nc.sbuf_tensor("s1kvacc", [64, NH * 65], F32))
    ks16 = e(nc.sbuf_tensor("s1ks16", [64, NH], F32))
    kst = e(nc.sbuf_tensor("s1kst", [NH, 64], F32))
    ksrow = e(nc.sbuf_tensor("s1ksrow", [1, D], F32))
    id_sb = e(nc.sbuf_tensor("s1id", [64, 64], F32))
    qkv_ps = [e(nc.psum_tensor(f"s1qp{i}", [P, 512], F32)) for i in range(2)]
    kv_ps = e(nc.psum_tensor("s1kvp", [64, NH * P], F32))
    tr_ps = e(nc.psum_tensor("s1trp", [NH, 64], F32))

    V, S, G, T_, Y = ctx.make_steps()
    ld, pe, vs, ac, gp, stq = (ctx.sem[k] for k in ["ld", "pe", "vs", "ac", "gp", "stq"])

    ldid = ctx.new_sem("ldid")
    stqf = [ctx.new_sem(f"stqf{i}") for i in range(2)]

    ldid.n += 16

    def _yid(y):
        y.dma_start(out=id_sb[:], in_=d["ident"][0:64, 0:64]).then_inc(ldid.h, 16)
    Y(_yid, mile=False)

    def _vinit(v):
        v.memset(kv_acc[:], 0.0)
        v.memset(kv2[:], 0.0)
    kvacc_init = V(_vinit)

    triples = [(d["wqkvT"][i * P:(i + 1) * P, :], wstage[i % 2][:], w_bf[i][:]) for i in range(8)]
    triples += [(d["xT"][i * P:(i + 1) * P, :], xstage[i % 2][:], x_bf[i][:]) for i in range(8)]
    conv_m = _stream_loads(ctx, Y, G, triples, "s1w")
    all_conv = conv_m[-1]

    bank_free = {}
    qf_store_n = {}
    kv_mm_prev = None
    qf_ready = None
    for tc in range(NT):
        kread = None
        vh_done = None
        for ct in range(6):
            b = ct % 2
            path = "q" if ct < 2 else ("k" if ct < 4 else "v")
            gi = tc * 6 + ct
            w = []
            if gi == 0:
                w.append((gp, all_conv))
            if gi >= 2:
                w.append((vs, bank_free[gi - 2]))

            def _mm(t, tc=tc, ct=ct, b=b):
                for i in range(8):
                    t.matmul(qkv_ps[b][:], x_bf[i][:, tc * P:(tc + 1) * P],
                             w_bf[i][:, ct * 512:(ct + 1) * 512],
                             start=(i == 0), stop=(i == 7))
            pe_mm = T_(_mm, waits=w)

            if path in ("q", "k"):
                def _s1(s, b=b):
                    s.activation(sqf[:], qkv_ps[b][:], AF.Square)
                sw = [(pe, pe_mm)]
                if path == "q" and tc >= 2:
                    sw.append((stqf[tc % 2], qf_store_n[tc - 2]))
                if kv_mm_prev is not None:
                    sw.append((pe, kv_mm_prev))
                a1 = S(_s1, waits=sw)

                def _v1(v):
                    v.reduce_sum(sq16[:], sqf[:].rearrange("p (h z) -> p h z", z=HD), axis=AX.X)
                    v.tensor_scalar_max(sq16[:], sq16[:], L2_EPS2)
                v1 = V(_v1, waits=[(ac, a1)])

                def _s2(s):
                    s.activation(rn16[:], sq16[:], AF.Ln)
                    s.activation(rn16[:], rn16[:], AF.Exp, scale=-0.5)
                a2 = S(_s2, waits=[(vs, v1)])

                def _v2(v, b=b):
                    in0 = qkv_ps[b][:].rearrange("p (h z) -> p h z", z=HD)
                    in1 = rn16[:].rearrange("p (h o) -> p h o", o=1)
                    in1b, in0b = bass.broadcast_tensor_aps(in1, in0)
                    v.tensor_tensor(qn[:].rearrange("p (h z) -> p h z", z=HD), in0b, in1b, ALU.mult)
                v2 = V(_v2, waits=[(ac, a2)])
                bank_free[gi] = v2

                def _s3(s, tc=tc, ct=ct, path=path):
                    s.activation(esb[:], qn[:], AF.Exp)
                    dst = qf_sb[tc % 2] if path == "q" else kf_sb
                    off = (ct % 2) * 512
                    s.activation(dst[:, off:off + 512], esb[:], AF.Ln, bias=1.0)
                a3 = S(_s3, waits=[(vs, v2)])
                if path == "q" and ct == 1:
                    qf_ready = a3
                if path == "k" and ct == 3:
                    kread = a3
            else:
                def _v3(v, b=b, ct=ct, tc=tc):
                    if tc == 0 and ct == 4:
                        for h in range(NH):
                            v.memset(vh_sb[:, h * 65 + 64:h * 65 + 65], 1.0)
                    off_h = (ct - 4) * 8
                    out = vh_sb[:, off_h * 65:(off_h + 8) * 65].rearrange(
                        "p (h z) -> p h z", z=65)[:, :, 0:HD]
                    v.tensor_copy(out, qkv_ps[b][:].rearrange("p (h z) -> p h z", z=HD))
                vw = [(pe, pe_mm)]
                if kv_mm_prev is not None:
                    vw.append((pe, kv_mm_prev))
                v3 = V(_v3, waits=vw)
                bank_free[gi] = v3
                if ct == 5:
                    vh_done = v3

        kvw = [(ac, kread), (vs, vh_done)]
        if tc > 0:
            kvw.append((vs, kvacc_m))
        else:
            kvw.append((vs, kvacc_init))

        def _kv(t, tc=tc):
            for h in range(NH):
                t.matmul(kv_ps[0:64, h * P:h * P + 65],
                         kf_sb[:, h * HD:(h + 1) * HD],
                         vh_sb[:, h * 65:(h + 1) * 65],
                         start=(h % 4 == 0), stop=(h % 4 == 3))
        kv_mm_prev = T_(_kv, waits=kvw)

        def _vkv(v):
            v.tensor_tensor(kv_acc[:].rearrange("p (h z) -> p h z", z=65),
                            kv_acc[:].rearrange("p (h z) -> p h z", z=65),
                            kv_ps[0:64, :].rearrange("p (h z) -> p h z", z=P)[:, :, 0:65],
                            ALU.add)
        kvacc_m = V(_vkv, waits=[(pe, kv_mm_prev)])

        stqf[tc % 2].n += 16
        qf_store_n[tc] = stqf[tc % 2].n

        def _gst(g, tc=tc, qf_ready=qf_ready):
            g.wait_ge(ac.h, qf_ready)
            g.dma_start(out=d["qf_hbm"][tc * P:(tc + 1) * P, :],
                        in_=qf_sb[tc % 2][:]).then_inc(stqf[tc % 2].h, 16)
        G(_gst, mile=False)

    def _vep(v):
        kvv = kv_acc[:].rearrange("p (h z) -> p h z", z=65)
        out3 = kv2[0:64, :].rearrange("p (h z) -> p h z", z=P)
        v.tensor_scalar_mul(out3[:, :, 0:65], kvv, float(HD) ** -0.5)
        v.tensor_scalar_add(out3[:, :, 64:65], kvv[:, :, 64:65], EPS_LA)
        v.tensor_copy(kv2[64:128, :], kv2[0:64, :])
        v.tensor_copy(ks16[:].rearrange("p (h o) -> p h o", o=1), out3[:, :, 64:65])
    vep = V(_vep, waits=[(vs, kvacc_m)])

    def _pet(t):
        t.wait_ge(ldid.h, 16)
        t.transpose(tr_ps[:], ks16[:], id_sb[:])
    pet = T_(_pet, waits=[(vs, vep)])

    def _vep2(v):
        v.tensor_copy(kst[:], tr_ps[:])
    vep2 = V(_vep2, waits=[(pe, pet)])

    n_qf0, n_qf1 = stqf[0].n, stqf[1].n
    stq.n += 48

    def _gep(g, n0=n_qf0, n1=n_qf1, nf=stq.n):
        g.wait_ge(stqf[0].h, n0)
        g.wait_ge(stqf[1].h, n1)
        g.dma_start(out=ksrow[:], in_=kst[:]).then_inc(stq.h, 16)
        g.wait_ge(stq.h, nf - 32)
        g.dma_start(out=d["ksrow_hbm"], in_=ksrow[:]).then_inc(stq.h, 16)
        g.dma_start(out=d["kv2_hbm"], in_=kv2[:]).then_inc(stq.h, 16)
        g.wait_ge(stq.h, nf)
    G(_gep, waits=[(vs, vep2)], mile=False)
    ctx.flush()



def emit_stage2(ctx, d, T):
    nc, e = ctx.nc, ctx.ex.enter_context
    NTile = T // 512
    kv2 = e(nc.sbuf_tensor("s2kv2", [P, 2048], F32))
    kv2b = e(nc.sbuf_tensor("s2kv2b", [P, 2048], BF16))
    ksrow = e(nc.sbuf_tensor("s2ksrow", [1, D], F32))
    ksb = e(nc.sbuf_tensor("s2ksb", [P, D], F32))
    s1c = e(nc.sbuf_tensor("s2s1c", [P, 32], F32))
    s2c = e(nc.sbuf_tensor("s2s2c", [P, 32], F32))
    sh1c = e(nc.sbuf_tensor("s2sh1c", [P, 32], F32))
    sh2c = e(nc.sbuf_tensor("s2sh2c", [P, 32], F32))
    s3c = e(nc.sbuf_tensor("s2s3c", [P, 8], F32))
    qf_in = [e(nc.sbuf_tensor(f"s2qfi{i}", [P, D], BF16)) for i in range(2)]
    prod = e(nc.sbuf_tensor("s2prod", [P, D], F32))
    den16 = e(nc.sbuf_tensor("s2den", [P, NH], F32))
    qfs_o = [e(nc.sbuf_tensor(f"s2qfs{i}", [P, D], BF16)) for i in range(2)]
    qfsT = [e(nc.sbuf_tensor(f"s2qT{i}", [P, 512], BF16)) for i in range(8)]
    xres = [e(nc.sbuf_tensor(f"s2xr{i}", [P, 512], F32)) for i in range(2)]
    y1f = [e(nc.sbuf_tensor(f"s2y1f{i}", [P, 512], F32)) for i in range(8)]
    y1b = [e(nc.sbuf_tensor(f"s2y1b{i}", [P, 512], BF16)) for i in range(8)]
    r2 = [e(nc.sbuf_tensor(f"s2r2_{i}", [P, 512], F32)) for i in range(8)]
    y2b = [e(nc.sbuf_tensor(f"s2y2b{i}", [P, 512], BF16)) for i in range(8)]
    abuf = [e(nc.sbuf_tensor(f"s2a{i}", [P, 2048], BF16)) for i in range(8)]
    bbuf = [e(nc.sbuf_tensor(f"s2b{i}", [P, 2048], BF16)) for i in range(8)]
    u1scr = [e(nc.sbuf_tensor(f"s2u1{i}", [P, 512], F32)) for i in range(2)]
    wstg = [e(nc.sbuf_tensor(f"s2ws{i}", [P, 1024], F32)) for i in range(2)]
    wbf = [e(nc.sbuf_tensor(f"s2wb{i}", [P, 1024], BF16)) for i in range(2)]
    ln = make_ln_scratch(nc, e, "s2ln")
    mm_ps = [e(nc.psum_tensor(f"s2mp{i}", [P, 512], F32)) for i in range(2)]

    V, S, G, T_, Y = ctx.make_steps()
    ld, pe, vs, ac, gp, stq = (ctx.sem[k] for k in ["ld", "pe", "vs", "ac", "gp", "stq"])
    ldw = [ctx.new_sem(f"ldw{i}") for i in range(2)]
    ldq = [ctx.new_sem(f"ldq{i}") for i in range(2)]
    ldx = [ctx.new_sem(f"ldx{i}") for i in range(2)]
    ldt = ctx.new_sem("ldt")
    ldpre = ctx.new_sem("ldpre")
    stqf = [ctx.new_sem(f"s2stqf{i}") for i in range(2)]
    stout = ctx.new_sem("stout")

    def a_chunk(m):
        return abuf[m // 4][:, (m % 4) * 512:(m % 4) * 512 + 512]

    def b_chunk(m):
        return bbuf[m // 4][:, (m % 4) * 512:(m % 4) * 512 + 512]

    ldpre.n += 7 * 16
    pre_ld = ldpre.n

    def _y0(y):
        for src, dst in [(d["kv2_hbm"], kv2[:]), (d["ksrow_hbm"], ksrow[:]),
                         (d["s1c"], s1c[:]), (d["s2c"], s2c[:]), (d["sh1c"], sh1c[:]),
                         (d["sh2c"], sh2c[:]), (d["s3c"], s3c[:])]:
            y.dma_start(out=dst, in_=src).then_inc(ldpre.h, 16)
    Y(_y0, mile=False)

    def _v0(v):
        v.memset(ln["ones"][:], 1.0)
    v0 = V(_v0)

    def _g0(g):
        g.wait_ge(ldpre.h, pre_ld)
        g.tensor_copy(kv2b[:], kv2[:])
    g0 = G(_g0)

    def _t0(t):
        t.wait_ge(ldpre.h, pre_ld)
        t.matmul(ln["bc_r"][:, 0:512], ln["ones"][0:1, :], ksrow[0:1, 0:512], start=True, stop=True)
        t.matmul(ln["bc_m"][:, 0:512], ln["ones"][0:1, :], ksrow[0:1, 512:1024], start=True, stop=True)
    t0 = T_(_t0, waits=[(vs, v0)])

    def _v01(v):
        v.tensor_copy(ksb[:, 0:512], ln["bc_r"][:, 0:512])
        v.tensor_copy(ksb[:, 512:1024], ln["bc_m"][:, 0:512])
    v01 = V(_v01, waits=[(pe, t0)])

    wconv = [None, None]
    wuse = [None, None]

    def queue_wload(src_ap, ncols, slot):
        ldw[slot].n += 16
        need = ldw[slot].n

        def _yw(y, src_ap=src_ap, ncols=ncols, slot=slot, wc=wconv[slot]):
            if wc is not None:
                y.wait_ge(gp.h, wc)
            y.dma_start(out=wstg[slot][:, 0:ncols], in_=src_ap).then_inc(ldw[slot].h, 16)
        Y(_yw, mile=False)

        def _gw(g, ncols=ncols, slot=slot, need=need, wu=wuse[slot]):
            g.wait_ge(ldw[slot].h, need)
            if wu is not None:
                g.wait_ge(pe.h, wu)
            g.tensor_copy(wbf[slot][:, 0:ncols], wstg[slot][:, 0:ncols])
        wconv[slot] = G(_gw)
        return wconv[slot]

    qf_used = {}
    qfs_done = {}
    xres_used = {}
    latt_used = {}
    evac_free = [None, None]
    y2b_store = {}
    r2_store = {}
    glob_j = 0

    for ti in range(NTile):
        for j in range(4):
            tc = ti * 4 + j
            gj = glob_j

            ldq[gj % 2].n += 16
            qld = ldq[gj % 2].n

            def _yq(y, tc=tc, gj=gj):
                if gj >= 2:
                    y.wait_ge(vs.h, qf_used[gj - 2])
                y.dma_start(out=qf_in[gj % 2][:],
                            in_=d["qf_hbm"][tc * P:(tc + 1) * P, :]).then_inc(ldq[gj % 2].h, 16)
            Y(_yq, mile=False)

            w = [(ldq[gj % 2], qld)]
            if gj == 0:
                w.append((vs, v01))
            if gj >= 2:
                w.append(qfs_done[gj - 2])

            def _vq(v, gj=gj):
                v.tensor_tensor(prod[:], qf_in[gj % 2][:], ksb[:], ALU.mult)
                v.reduce_sum(den16[:], prod[:].rearrange("p (h z) -> p h z", z=HD), axis=AX.X)
                v.reciprocal(den16[:], den16[:])
                in0 = qf_in[gj % 2][:].rearrange("p (h z) -> p h z", z=HD)
                in1 = den16[:].rearrange("p (h o) -> p h o", o=1)
                in1b, in0b = bass.broadcast_tensor_aps(in1, in0)
                v.tensor_tensor(qfs_o[gj % 2][:].rearrange("p (h z) -> p h z", z=HD),
                                in0b, in1b, ALU.mult)
            vq = V(_vq, waits=w)
            qf_used[gj] = vq

            stqf[gj % 2].n += 16
            qfs_done[gj] = (stqf[gj % 2], stqf[gj % 2].n)

            def _gq(g, tc=tc, gj=gj, vq=vq):
                g.wait_ge(vs.h, vq)
                g.dma_start(out=d["qfs_hbm"][tc * P:(tc + 1) * P, :],
                            in_=qfs_o[gj % 2][:]).then_inc(stqf[gj % 2].h, 16)
            G(_gq, mile=False)
            glob_j += 1
        all_qfs = [(stqf[0], stqf[0].n), (stqf[1], stqf[1].n)]

        ldt.n += 8 * 16
        t_ld = ldt.n

        def _yt(y, ti=ti, all_qfs=all_qfs):
            for sem, val in all_qfs:
                y.wait_ge(sem.h, val)
            if ti > 0:
                y.wait_ge(pe.h, latt_used[ti - 1])
            for c in range(8):
                y.dma_start(out=qfsT[c][:],
                            in_=d["qfs_hbm"][ti * 512:(ti + 1) * 512, c * P:(c + 1) * P],
                            transpose=True).then_inc(ldt.h, 16)
        Y(_yt, mile=False)

        y1_ready = {}
        for cp in range(8):
            ldx[cp % 2].n += 16
            x_ld = ldx[cp % 2].n

            def _yx(y, cp=cp, ti=ti):
                k = ti * 8 + cp
                if k >= 2:
                    y.wait_ge(vs.h, xres_used[k - 2])
                y.dma_start(out=xres[cp % 2][:],
                            in_=d["xT"][cp * P:(cp + 1) * P, ti * 512:(ti + 1) * 512]
                            ).then_inc(ldx[cp % 2].h, 16)
            Y(_yx, mile=False)

            w = [(ldt, t_ld)]
            if evac_free[cp % 2] is not None:
                w.append(evac_free[cp % 2])

            def _tl(t, cp=cp):
                for par in range(2):
                    h = 2 * cp + par
                    bs = (h % 2) * 64
                    t.matmul(mm_ps[cp % 2][bs:bs + 64, :],
                             kv2b[bs:bs + 64, h * P:h * P + 64],
                             qfsT[cp][bs:bs + 64, :],
                             start=True, stop=True)
            if ti == 0 and cp == 0:
                w.append((gp, g0))
            tl = T_(_tl, waits=w)

            def _vr(v, cp=cp, x_ld=x_ld, tl=tl):
                v.wait_ge(ldx[cp % 2].h, x_ld)
                v.wait_ge(pe.h, tl)
                v.tensor_tensor(y1f[cp][:], mm_ps[cp % 2][:], xres[cp % 2][:], ALU.add)
            vr = V(_vr, waits=[(vs, ln1_done[cp]) if ti > 0 else (vs, None)])
            evac_free[cp % 2] = (vs, vr)
            xres_used[ti * 8 + cp] = vr
            y1_ready[cp] = vr
        latt_used[ti] = pe.n

        cw = [[(vs, y1_ready[c])] for c in range(8)]
        bw = [[(pe, gate_done)] if ti > 0 else [] for c in range(8)]
        ln1_done = emit_ln(ctx, V, S, T_,
                           [y1f[c][:] for c in range(8)],
                           [y1b[c][:] for c in range(8)],
                           ln, chunk_waits=cw, ybf_waits=bw)

        u1_used = [None, None]
        a_done = {}
        for m in range(32):
            c1 = queue_wload(d["w1Tr"][:, :, m * P:(m + 1) * P], 1024, 0)

            w = [(gp, c1)]
            if evac_free[0] is not None:
                w.append(evac_free[0])
            if m == 0:
                w += [(vs, x) for x in ln1_done]

            def _t1(t):
                for i in range(8):
                    t.matmul(mm_ps[0][:], wbf[0][:, i * P:(i + 1) * P], y1b[i][:],
                             start=(i == 0), stop=(i == 7))
            t1 = T_(_t1, waits=w)
            wuse[0] = t1

            w = [(pe, t1)]
            if u1_used[m % 2] is not None:
                w.append((vs, u1_used[m % 2]))

            def _s1(s, m=m):
                s.activation(u1scr[m % 2][:], mm_ps[0][:], AF.Silu, scale=s1c[:, m:m + 1])
            ev1 = S(_s1, waits=w)
            evac_free[0] = (ac, ev1)

            c2 = queue_wload(d["w2Tr"][:, :, m * P:(m + 1) * P], 1024, 1)
            w = [(gp, c2)]
            if evac_free[1] is not None:
                w.append(evac_free[1])

            def _t2(t):
                for i in range(8):
                    t.matmul(mm_ps[1][:], wbf[1][:, i * P:(i + 1) * P], y1b[i][:],
                             start=(i == 0), stop=(i == 7))
            t2 = T_(_t2, waits=w)
            wuse[1] = t2

            def _v2(v, m=m):
                v.scalar_tensor_tensor(a_chunk(m), mm_ps[1][:], s2c[:, m:m + 1],
                                       u1scr[m % 2][:], ALU.mult, ALU.mult)
            v2 = V(_v2, waits=[(pe, t2), (ac, ev1)])
            evac_free[1] = (vs, v2)
            u1_used[m % 2] = v2
            a_done[m] = v2
        gate_done = pe.n

        c_all_done = None
        for layer in range(2):
            wname = "h1Tr" if layer == 0 else "h2Tr"
            scol = sh1c if layer == 0 else sh2c
            prev_evac = evac_free[0] if layer == 0 else h1_all
            for m in range(32):
                mm_here = None
                for sub in range(4):
                    csub = queue_wload(d[wname][:, 8 * sub:8 * sub + 8, m * P:(m + 1) * P], 1024, sub % 2)
                    w = [(gp, csub)]
                    if sub == 0:
                        if prev_evac is not None:
                            w.append(prev_evac)
                        if m == 0 and layer == 0:
                            w.append((vs, a_done[31]))

                    def _tm(t, sub=sub, layer=layer):
                        for i in range(8):
                            t.matmul(mm_ps[0][:], wbf[sub % 2][:, i * P:(i + 1) * P],
                                     (a_chunk(8 * sub + i) if layer == 0 else b_chunk(8 * sub + i)),
                                     start=(sub == 0 and i == 0), stop=(sub == 3 and i == 7))
                    mm_here = T_(_tm, waits=w)
                    wuse[sub % 2] = mm_here

                def _se(s, m=m, layer=layer, scol=scol):
                    s.activation((b_chunk(m) if layer == 0 else a_chunk(m)),
                                 mm_ps[0][:], AF.Gelu, scale=scol[:, m:m + 1])
                ev = S(_se, waits=[(pe, mm_here)])
                prev_evac = (ac, ev)
            if layer == 0:
                h1_all = prev_evac
            else:
                c_all_done = prev_evac

        prev_evac4 = None
        r2_ready = {}
        for m in range(8):
            mm_here = None
            for sub in range(4):
                csub = queue_wload(d["w3Tr"][:, 8 * sub:8 * sub + 8, m * P:(m + 1) * P], 1024, sub % 2)
                w = [(gp, csub)]
                if sub == 0:
                    if prev_evac4 is not None:
                        w.append(prev_evac4)
                    if m == 0:
                        w.append(c_all_done)

                def _tm(t, sub=sub):
                    for i in range(8):
                        t.matmul(mm_ps[0][:], wbf[sub % 2][:, i * P:(i + 1) * P],
                                 a_chunk(8 * sub + i),
                                 start=(sub == 0 and i == 0), stop=(sub == 3 and i == 7))
                mm_here = T_(_tm, waits=w)
                wuse[sub % 2] = mm_here

            w = [(pe, mm_here)]
            if ti > 0:
                w.append((stout, prev_tile_stores))

            def _ve(v, m=m):
                v.scalar_tensor_tensor(r2[m][:], mm_ps[0][:], s3c[:, m:m + 1],
                                       y1f[m][:], ALU.mult, ALU.add)
            ev = V(_ve, waits=w)
            prev_evac4 = (vs, ev)
            r2_ready[m] = ev
        evac_free[0] = prev_evac4

        cw = [[(vs, r2_ready[c])] for c in range(8)]
        bw = [[(stout, prev_tile_stores)] if ti > 0 else [] for c in range(8)]
        ln2_done = emit_ln(ctx, V, S, T_,
                           [r2[c][:] for c in range(8)],
                           [y2b[c][:] for c in range(8)],
                           ln, chunk_waits=cw, ybf_waits=bw)

        for c in range(8):
            stout.n += 32

            def _gst(g, c=c, ti=ti):
                g.wait_ge(vs.h, ln2_done[c])
                g.dma_start(out=d["y2f_hbm"][c * P:(c + 1) * P, ti * 512:(ti + 1) * 512],
                            in_=r2[c][:]).then_inc(stout.h, 16)
                g.dma_start(out=d["y2b_hbm"][c * P:(c + 1) * P, ti * 512:(ti + 1) * 512],
                            in_=y2b[c][:]).then_inc(stout.h, 16)
            G(_gst, mile=False)
        prev_tile_stores = stout.n

    def _gfin(g, a=stout.n, b=stqf[0].n, c=stqf[1].n):
        g.wait_ge(stout.h, a)
        g.wait_ge(stqf[0].h, b)
        g.wait_ge(stqf[1].h, c)
    G(_gfin, mile=False)
    ctx.flush()




def emit_stage3(ctx, d, T):
    nc, e = ctx.nc, ctx.ex.enter_context
    NT2 = T // 512
    wstage = [e(nc.sbuf_tensor(f"s3ws{i}", [P, D], F32)) for i in range(2)]
    weffbf = [[e(nc.sbuf_tensor(f"s3we{p}_{i}", [P, D], BF16)) for i in range(8)] for p in range(3)]
    wefftmp = e(nc.sbuf_tensor("s3wet", [P, D], BF16))
    a_sb = [e(nc.sbuf_tensor(f"s3a{p}", [8, D], F32)) for p in range(4)]
    bt_sb = [e(nc.sbuf_tensor(f"s3bt{p}", [8, D], F32)) for p in range(4)]
    y2t = [e(nc.sbuf_tensor(f"s3y2{i}", [P, 512], BF16)) for i in range(8)]
    qk_bf = [e(nc.sbuf_tensor(f"s3qk{i}", [P, 512], BF16)) for i in range(2)]
    qsq_bf = [e(nc.sbuf_tensor(f"s3qs{i}", [P, 512], BF16)) for i in range(2)]
    hpat = [e(nc.sbuf_tensor(f"s3hp{i}", [P, NH], BF16)) for i in range(8)]
    mx16 = e(nc.sbuf_tensor("s3mx", [NH, 2], F32))
    tmp16 = e(nc.sbuf_tensor("s3tq", [NH, 1], F32))
    bq8 = e(nc.sbuf_tensor("s3bq8", [P, 8], F32))
    bkc = e(nc.sbuf_tensor("s3bkc", [P, 8], F32))
    bvrow = e(nc.sbuf_tensor("s3bvr", [1, D], F32))
    bvb = e(nc.sbuf_tensor("s3bvb", [P, D], F32))
    vh_sb = [e(nc.sbuf_tensor(f"s3vh{i}", [P, NH * 65], BF16)) for i in range(2)]
    id_sb = e(nc.sbuf_tensor("s3id", [NH, NH], F32))
    ones_sb = e(nc.sbuf_tensor("s3on", [1, P], F32))
    negarow = e(nc.sbuf_tensor("s3neg", [1, NH], F32))
    lora_ps = e(nc.psum_tensor("s3lp", [P, D], F32))
    qk_ps = [e(nc.psum_tensor(f"s3qp{i}", [P, 512], F32)) for i in range(2)]
    qs_ps = [e(nc.psum_tensor(f"s3sp{i}", [NH, 512], F32)) for i in range(2)]
    v_ps = [e(nc.psum_tensor(f"s3vp{i}", [P, 512], F32)) for i in range(2)]

    V, S, G, T_, Y = ctx.make_steps()
    ld, pe, vs, ac, gp, stq = (ctx.sem[k] for k in ["ld", "pe", "vs", "ac", "gp", "stq"])
    ldpre = ctx.new_sem("s3pre")
    ldw = [ctx.new_sem(f"s3lw{i}") for i in range(2)]
    ldy = ctx.new_sem("s3ly")
    stqk = [ctx.new_sem(f"s3sq{i}") for i in range(2)]
    stv = [ctx.new_sem(f"s3sv{i}") for i in range(2)]
    names = ["q", "k", "v", "o"]

    ldpre.n += 12 * 16
    pre_n = ldpre.n

    def _y0(y):
        for p in range(4):
            y.dma_start(out=a_sb[p][:], in_=d["a_" + names[p]]).then_inc(ldpre.h, 16)
            y.dma_start(out=bt_sb[p][:], in_=d["bt_" + names[p]]).then_inc(ldpre.h, 16)
        for src, dst in [(d["bq_c"], bq8[:]), (d["bk_c"], bkc[:]),
                         (d["bv_row"], bvrow[:]), (d["ident"][0:NH, 0:NH], id_sb[:])]:
            y.dma_start(out=dst, in_=src).then_inc(ldpre.h, 16)
    Y(_y0, mile=False)

    def _v0(v):
        v.wait_ge(ldpre.h, pre_n)
        v.tensor_scalar_mul(bq8[:], bq8[:], ATT_SCALE)
        v.memset(ones_sb[:], 1.0)
        v.memset(mx16[:], 0.0)
        for c in range(8):
            v.memset(hpat[c][:], 0.0)
        for i in range(2):
            v.memset(vh_sb[i][:].rearrange("p (h z) -> p h z", z=65)[:, :, 64:65], 1.0)
        for c in range(8):
            v.memset(hpat[c][0:64, 2 * c:2 * c + 1], 1.0)
            v.memset(hpat[c][64:128, 2 * c + 1:2 * c + 2], 1.0)
    v0 = V(_v0)

    evac_prev = {}
    for p in range(4):
        for i in range(8):
            k = p * 8 + i
            slot = k % 2

            ldw[slot].n += 16
            wn = ldw[slot].n

            def _yw(y, p=p, i=i, slot=slot, k=k, ep=evac_prev.get(k - 2)):
                if k >= 2 and ep is not None:
                    y.wait_ge(vs.h, ep)
                y.dma_start(out=wstage[slot][:],
                            in_=d["w_" + names[p]][i * P:(i + 1) * P, :]).then_inc(ldw[slot].h, 16)
            Y(_yw, mile=False)

            w = [(ldpre, pre_n)] if k == 0 else []
            if k >= 1:
                w.append((vs, evac_prev.get(k - 1)))

            def _tm(t, p=p, i=i):
                for m in range(8):
                    t.matmul(lora_ps[:, m * P:(m + 1) * P],
                             a_sb[p][0:8, i * P:(i + 1) * P],
                             bt_sb[p][0:8, m * P:(m + 1) * P],
                             start=True, stop=True)
            tm = T_(_tm, waits=w)

            def _ve(v, p=p, i=i, slot=slot, wn=wn, tm=tm):
                v.wait_ge(ldw[slot].h, wn)
                v.wait_ge(pe.h, tm)
                dst = weffbf[p][i][:] if p < 3 else wefftmp[:]
                v.scalar_tensor_tensor(dst, lora_ps[:], 2.0, wstage[slot][:],
                                       ALU.mult, ALU.add)
            ve = V(_ve)
            evac_prev[k] = ve

            if p == 3:
                stq.n += 16

                def _go(g, i=i, ve=ve):
                    g.wait_ge(vs.h, ve)
                    g.dma_start(out=d["weffoT_hbm"][i * P:(i + 1) * P, :],
                                in_=wefftmp[:]).then_inc(stq.h, 16)
                G(_go, mile=False)
                evac_prev[k] = None

                def _vg(v, n=stq.n):
                    v.wait_ge(stq.h, n)
                vgm = V(_vg)
                evac_prev[k] = vgm

    def _tb(t):
        t.matmul(lora_ps[:, 0:512], ones_sb[0:1, 0:P], bvrow[0:1, 0:512], start=True, stop=True)
        t.matmul(lora_ps[:, 512:1024], ones_sb[0:1, 0:P], bvrow[0:1, 512:1024], start=True, stop=True)
    tb = T_(_tb, waits=[(vs, evac_prev[31])])

    def _vb(v):
        v.tensor_copy(bvb[:], lora_ps[:])
    vb = V(_vb, waits=[(pe, tb)])

    y2_used = None
    sq_prev = {}
    th_prev = {}
    st_prev = {}
    stats_done = {0: None, 1: None}
    v_bank = {}
    vh_guard = [None, None]
    for tt in range(NT2):
        ldy.n += 8 * 16
        yn = ldy.n

        def _yy(y, tt=tt, yu=y2_used):
            if yu is not None:
                y.wait_ge(pe.h, yu)
            for i in range(8):
                y.dma_start(out=y2t[i][:],
                            in_=d["y2b_hbm"][i * P:(i + 1) * P, tt * 512:(tt + 1) * 512]
                            ).then_inc(ldy.h, 16)
        Y(_yy, mile=False)

        for pk in range(2):
            wname = weffbf[pk]
            th = None
            for m in range(8):
                pb = m % 2
                w = [(ldy, yn)]
                if (tt, pk, m) == (0, 0, 0):
                    w.append((vs, v0))
                if sq_prev.get(pb) is not None:
                    w.append((vs, sq_prev[pb][0]))
                    w.append((ac, sq_prev[pb][1]))

                def _tq(t, pk=pk, m=m, pb=pb, wname=wname):
                    for i in range(8):
                        t.matmul(qk_ps[pb][:], wname[i][:, m * P:(m + 1) * P], y2t[i][:],
                                 start=(i == 0), stop=(i == 7))
                tq = T_(_tq, waits=w)

                w = [(pe, tq)]
                if th_prev.get(pb) is not None:
                    w.append((pe, th_prev[pb]))
                if st_prev.get(pb) is not None:
                    w.append(st_prev[pb])

                def _vq2(v, pk=pk, m=m, pb=pb):
                    bias = bq8 if pk == 0 else bkc
                    sc = ATT_SCALE if pk == 0 else 1.0
                    v.tensor_scalar(qk_bf[pb][:], qk_ps[pb][:], sc, bias[:, m:m + 1],
                                    ALU.mult, ALU.add)
                vq2 = V(_vq2, waits=w)

                def _sq(s, pb=pb):
                    s.activation(qsq_bf[pb][:], qk_bf[pb][:], AF.Square)
                sq = S(_sq, waits=[(vs, vq2)])
                sq_prev[pb] = (vq2, sq)

                stqk[pb].n += 16
                st_prev[pb] = (stqk[pb], stqk[pb].n)

                def _gq(g, pk=pk, m=m, tt=tt, pb=pb, vq2=vq2):
                    g.wait_ge(vs.h, vq2)
                    dst = d["q_hbm"] if pk == 0 else d["k_hbm"]
                    g.dma_start(out=dst[m * P:(m + 1) * P, tt * 512:(tt + 1) * 512],
                                in_=qk_bf[pb][:]).then_inc(stqk[pb].h, 16)
                G(_gq, mile=False)

                w = [(ac, sq)]
                if m == 0 and stats_done[pk] is not None:
                    w.append((vs, stats_done[pk]))

                def _th(t, pk=pk, m=m, pb=pb):
                    t.matmul(qs_ps[pk][:], hpat[m][:], qsq_bf[pb][:],
                             start=(m == 0), stop=(m == 7))
                th = T_(_th, waits=w)
                th_prev[pb] = th

            def _vm(v, pk=pk, th=th):
                v.reduce_max(tmp16[:], qs_ps[pk][:], axis=AX.X)
                v.tensor_tensor(mx16[:, pk:pk + 1], mx16[:, pk:pk + 1], tmp16[:], ALU.max)
            stats_done[pk] = V(_vm, waits=[(pe, th)])

        for tsub in range(4):
            vv = None
            for ct in range(2):
                pb = ct
                w = []
                if v_bank.get(pb) is not None:
                    w.append((vs, v_bank[pb]))
                if (tt, tsub, ct) == (0, 0, 0):
                    w.append((vs, vb))
                w.append((ldy, yn))

                def _tv(t, tsub=tsub, ct=ct, pb=pb):
                    for i in range(8):
                        t.matmul(v_ps[pb][:], y2t[i][:, tsub * P:(tsub + 1) * P],
                                 weffbf[2][i][:, ct * 512:(ct + 1) * 512],
                                 start=(i == 0), stop=(i == 7))
                tv = T_(_tv, waits=w)

                w = [(pe, tv)]
                if ct == 0 and vh_guard[tsub % 2] is not None:
                    w.append(vh_guard[tsub % 2])

                def _vv(v, tsub=tsub, ct=ct, pb=pb):
                    dst = vh_sb[tsub % 2][:].rearrange("p (h z) -> p h z", z=65)[:, ct * 8:(ct + 1) * 8, 0:HD]
                    v.tensor_tensor(dst, v_ps[pb][:].rearrange("p (h z) -> p h z", z=HD),
                                    bvb[:, ct * 512:(ct + 1) * 512].rearrange("p (h z) -> p h z", z=HD),
                                    ALU.add)
                vv = V(_vv, waits=w)
                v_bank[pb] = vv

            stv[tsub % 2].n += 16
            vh_guard[tsub % 2] = (stv[tsub % 2], stv[tsub % 2].n)

            def _gv(g, tt=tt, tsub=tsub, vv=vv):
                g.wait_ge(vs.h, vv)
                t0 = tt * 512 + tsub * P
                g.dma_start(out=d["vhat_hbm"][:, t0:t0 + P, :].rearrange("h t f -> t h f"),
                            in_=vh_sb[tsub % 2][:].rearrange("p (h z) -> p h z", z=65)
                            ).then_inc(stv[tsub % 2].h, 16)
            G(_gv, mile=False)
        y2_used = pe.n

    def _vc(v):
        v.tensor_tensor(tmp16[:], mx16[:, 0:1], mx16[:, 1:2], ALU.mult)
        v.tensor_scalar_max(tmp16[:], tmp16[:], 1e-30)
    vc = V(_vc, waits=[(vs, stats_done[0]), (vs, stats_done[1])])

    def _sc(s):
        s.activation(tmp16[:], tmp16[:], AF.Ln)
        s.activation(tmp16[:], tmp16[:], AF.Exp, scale=0.5)
    sc = S(_sc, waits=[(vs, vc)])

    def _vn(v):
        v.tensor_scalar_mul(tmp16[:], tmp16[:], -1.0)
    vn = V(_vn, waits=[(ac, sc)])

    def _tc(t):
        t.transpose(qs_ps[0][0:1, 0:NH], tmp16[:], id_sb[:])
    tc = T_(_tc, waits=[(vs, vn)])

    def _vn2(v):
        v.tensor_copy(negarow[:], qs_ps[0][0:1, 0:NH])
    vn2 = V(_vn2, waits=[(pe, tc)])

    stq.n += 16

    def _gf(g, nf=stq.n, fins=[(sm, sm.n) for sm in [stqk[0], stqk[1], stv[0], stv[1]]]):
        g.dma_start(out=d["negarow_hbm"], in_=negarow[:]).then_inc(stq.h, 16)
        g.wait_ge(stq.h, nf)
        for sm, val in fins:
            g.wait_ge(sm.h, val)
    G(_gf, waits=[(vs, vn2)], mile=False)
    ctx.flush()




def emit_stage4(ctx, d, T):
    nc, e = ctx.nc, ctx.ex.enter_context
    NQO = T // 1024
    NKC = T // P
    NKQ = max(1, T // 1024)
    NKT = min(4, NKQ)
    q_sb = [e(nc.sbuf_tensor(f"s4q{i}", [P, 512], BF16)) for i in range(8)]
    ktile4 = [e(nc.sbuf_tensor(f"s4k{i}", [P, 1024], BF16)) for i in range(NKT)]
    vh_t = [e(nc.sbuf_tensor(f"s4v{i}", [P, NKC * 65], BF16)) for i in range(2)]
    pp_bf = [e(nc.sbuf_tensor(f"s4pp{i}", [P, 512], BF16)) for i in range(4)]
    att_bf = [e(nc.sbuf_tensor(f"s4at{i}", [P, 512], BF16)) for i in range(8)]
    rr_sb = e(nc.sbuf_tensor("s4rr", [1, 512], F32))
    rb_sb = e(nc.sbuf_tensor("s4rb", [64, 512], F32))
    acol = e(nc.sbuf_tensor("s4ac", [P, NH], F32))
    y3f = [e(nc.sbuf_tensor(f"s4y3f{i}", [P, 1024], F32)) for i in range(8)]
    y3b = [e(nc.sbuf_tensor(f"s4y3b{i}", [P, 1024], BF16)) for i in range(8)]
    y2res = [e(nc.sbuf_tensor(f"s4y2r{i}", [P, 512], F32)) for i in range(2)]
    y3scr = e(nc.sbuf_tensor("s4y3s", [P, 512], F32))
    act_e = [e(nc.sbuf_tensor(f"s4ae{i}", [P, 2048], BF16)) for i in range(8)]
    tmp_f = e(nc.sbuf_tensor("s4tmp", [P, 512], F32))
    u1scr = [e(nc.sbuf_tensor(f"s4u1{i}", [P, 512], F32)) for i in range(2)]
    wstg = [e(nc.sbuf_tensor(f"s4ws{i}", [P, D], F32)) for i in range(2)]
    wbf = [e(nc.sbuf_tensor(f"s4wb{i}", [P, D], BF16)) for i in range(2)]
    gw_bf = [e(nc.sbuf_tensor(f"s4gw{i}", [P, 8], BF16)) for i in range(8)]
    gwstg = e(nc.sbuf_tensor("s4gws", [P, 8], F32))
    gb_sb = e(nc.sbuf_tensor("s4gbs", [8, 1], F32))
    bo_c = e(nc.sbuf_tensor("s4bo", [P, 8], F32))
    b0c_sb = [e(nc.sbuf_tensor(f"s4b0_{i}", [P, 16], F32)) for i in range(NE)]
    b1c_sb = [e(nc.sbuf_tensor(f"s4b1_{i}", [P, 16], F32)) for i in range(NE)]
    b2stg = e(nc.sbuf_tensor("s4b2s", [8, D], F32))
    b2_sb = e(nc.sbuf_tensor("s4b2", [8, D], BF16))
    lg_sb = e(nc.sbuf_tensor("s4lg", [8, 512], F32))
    lt_sb = e(nc.sbuf_tensor("s4lt", [P, 8], F32))
    m1_sb = e(nc.sbuf_tensor("s4m1", [P, 1], F32))
    nm1_sb = e(nc.sbuf_tensor("s4nm1", [P, 1], F32))
    m2_sb = e(nc.sbuf_tensor("s4m2", [P, 1], F32))
    msk_sb = e(nc.sbuf_tensor("s4msk", [P, 8], F32))
    rden_sb = e(nc.sbuf_tensor("s4rd", [P, 1], F32))
    g_tm = e(nc.sbuf_tensor("s4gtm", [P, 8], F32))
    g_sb = e(nc.sbuf_tensor("s4gs", [8, 1024], F32))
    g_bf = e(nc.sbuf_tensor("s4gbf", [8, 1024], BF16))
    gflat = e(nc.sbuf_tensor("s4gf", [65, 4096], F32))
    gbv_sb = e(nc.sbuf_tensor("s4gbv", [P, 512], F32))
    nrow_sb = e(nc.sbuf_tensor("s4nr", [1, NH], F32))
    id_sb = e(nc.sbuf_tensor("s4id", [P, P], F32))
    ln = make_ln_scratch(nc, e, "s4ln")
    pa = [e(nc.psum_tensor(f"s4pa{i}", [P, 512], F32)) for i in range(2)]
    pb = [e(nc.psum_tensor(f"s4pb{i}", [65, 512], F32)) for i in range(2)]
    sp = [pa[0], pa[1], ln["bc_r"]]
    aux = ln["bc_m"]

    V, S, G, T_, Y = ctx.make_steps()
    ld, pe, vs, ac, gp, stq = (ctx.sem[k] for k in ["ld", "pe", "vs", "ac", "gp", "stq"])
    ldpre = ctx.new_sem("s4pre")
    ldq = ctx.new_sem("s4lq")
    ldk = [ctx.new_sem(f"s4lk{i}") for i in range(4)]
    ldv = [ctx.new_sem(f"s4lv{i}") for i in range(2)]
    ldy2 = [ctx.new_sem(f"s4ly{i}") for i in range(2)]
    ldw = [ctx.new_sem(f"s4lw{i}") for i in range(2)]
    gfq = ctx.new_sem("s4gfq")
    stout = ctx.new_sem("s4st")

    ldpre.n += (5 + 2 * NE) * 16
    pre_n = ldpre.n

    def _y0(y):
        for src, dst in [(d["negarow_hbm"], nrow_sb[:]), (d["ident"], id_sb[:]),
                         (d["bo_c"], bo_c[:]), (d["gb_c"], gb_sb[:]), (d["b2m"], b2stg[:])]:
            y.dma_start(out=dst, in_=src).then_inc(ldpre.h, 16)
        for ee in range(NE):
            y.dma_start(out=b0c_sb[ee][:], in_=d["b0c"][ee]).then_inc(ldpre.h, 16)
            y.dma_start(out=b1c_sb[ee][:], in_=d["b1c"][ee]).then_inc(ldpre.h, 16)
    Y(_y0, mile=False)

    def _g0(g):
        g.wait_ge(ldpre.h, pre_n)
        g.tensor_copy(b2_sb[:], b2stg[:])
    g0 = G(_g0)

    ldgw = ctx.new_sem("s4lgw")
    gwc = []
    for i in range(8):
        ldgw.n += 16
        nn = ldgw.n

        def _yg(y, i=i):
            if i >= 1:
                y.wait_ge(gp.h, gwc[i - 1])
            y.dma_start(out=gwstg[:], in_=d["gwT"][i * P:(i + 1) * P, :]).then_inc(ldgw.h, 16)
        Y(_yg, mile=False)

        def _gg(g, i=i, nn=nn):
            g.wait_ge(ldgw.h, nn)
            g.tensor_copy(gw_bf[i][:], gwstg[:])
        gwc.append(G(_gg))

    def _v0(v):
        v.wait_ge(ldpre.h, pre_n)
        v.memset(ln["ones"][:], 1.0)
    v0 = V(_v0)

    def _t0(t):
        t.matmul(aux[:, 0:NH], ln["ones"][0:1, :], nrow_sb[0:1, :], start=True, stop=True)
    t0 = T_(_t0, waits=[(vs, v0)])

    def _va(v):
        v.tensor_copy(acol[:], aux[:, 0:NH])
    va = V(_va, waits=[(pe, t0)])

    st = {
        "q_used": None, "k_used": [None] * NKT, "v_used": [None, None],
        "sp_free": {0: (vs, va), 1: None, 2: None},
        "pb_free": [None, None],
        "aux_free": (vs, va),
        "att_used": None,
        "moe_mm_last": None,
        "y2r_used": [None, None],
        "acc_guard": None,
        "u1_used": [None, None],
        "gbv_used": None,
        "tmp_used": None,
    }
    wconv = [None, None]
    wuse = [None, None]

    def queue_wload(src_ap, slot):
        ldw[slot].n += 16
        need = ldw[slot].n

        def _yw(y, src_ap=src_ap, slot=slot, wc=wconv[slot], wu=wuse[slot]):
            if wc is not None:
                y.wait_ge(gp.h, wc)
            if wu is not None:
                y.wait_ge(pe.h, wu)
            y.dma_start(out=wstg[slot][:], in_=src_ap).then_inc(ldw[slot].h, 16)
        Y(_yw, mile=False)

        def _gw(g, slot=slot, need=need, wu=wuse[slot]):
            g.wait_ge(ldw[slot].h, need)
            if wu is not None:
                g.wait_ge(pe.h, wu)
            g.tensor_copy(wbf[slot][:], wstg[slot][:])
        wconv[slot] = G(_gw)
        return wconv[slot]

    def last_kc_mod(r, nkc):
        for k in range(nkc - 1, -1, -1):
            if k % 3 == r:
                return k
        return None

    weffoTr = d["weffoT_hbm"].rearrange("(c p) m -> p c m", p=P)

    def queue_bfload(src_ap, slot):
        ldw[slot].n += 16
        need = ldw[slot].n

        def _yw(y, src_ap=src_ap, slot=slot, wc=wconv[slot], wu=wuse[slot]):
            if wu is not None:
                y.wait_ge(pe.h, wu)
            if wc is not None:
                y.wait_ge(gp.h, wc)
            y.dma_start(out=wbf[slot][:], in_=src_ap).then_inc(ldw[slot].h, 16)
        Y(_yw, mile=False)
        return need

    kload_n = {}
    for qo in range(NQO):
        g_done = {}
        lnd_half = {}
        for qh in range(2):
            qt = qo * 2 + qh
            ldq.n += 8 * 16
            qn_ld = ldq.n

            def _yq(y, qt=qt, qu=st["q_used"]):
                if qu is not None:
                    y.wait_ge(pe.h, qu)
                for i in range(8):
                    y.dma_start(out=q_sb[i][:],
                                in_=d["q_hbm"][i * P:(i + 1) * P, qt * 512:(qt + 1) * 512]
                                ).then_inc(ldq.h, 16)
            Y(_yq, mile=False)

            exp_m = {}
            pv_m = {}
            att_evac = {}
            prev_exp = {}
            prev_pv = {}
            for h in range(NH):
                bs = (h % 2) * 64
                if h % 2 == 0:
                    for kq in range(NKQ):
                        ldk[kq % 4].n += 16
                        kload_n[kq] = ldk[kq % 4].n

                        def _yk(y, h=h, kq=kq, ku=st["k_used"][kq]):
                            if ku is not None:
                                y.wait_ge(pe.h, ku)
                            y.dma_start(out=ktile4[kq][:],
                                        in_=d["k_hbm"][(h // 2) * P:(h // 2 + 1) * P,
                                                       kq * 1024:(kq + 1) * 1024]
                                        ).then_inc(ldk[kq % 4].h, 16)
                        Y(_yk, mile=False)
                vsl = h % 2

                ldv[vsl].n += 16
                vn_ld = ldv[vsl].n

                def _yv(y, h=h, vsl=vsl, vu=st["v_used"][vsl]):
                    if vu is not None:
                        y.wait_ge(pe.h, vu)
                    y.dma_start(out=vh_t[vsl][:].rearrange("p (c f) -> p c f", f=65),
                                in_=d["vhat_hbm"][h].rearrange("(c p) f -> p c f", p=P)
                                ).then_inc(ldv[vsl].h, 16)
                Y(_yv, mile=False)

                for kc in range(NKC):
                    bnk = kc % 3

                    kq = kc // 8

                    tw = [(ldk[kq % 4], kload_n[kq])]
                    if h == 0 and kc == 0:
                        tw.append((ldq, qn_ld))
                    if exp_m.get((h, kc - 3)) is not None:
                        tw.append((ac, exp_m[(h, kc - 3)]))
                    elif prev_exp.get(bnk) is not None:
                        tw.append((ac, prev_exp[bnk]))
                    elif st["sp_free"][bnk] is not None:
                        tw.append(st["sp_free"][bnk])

                    def _ts(t, h=h, kc=kc, kq=kq, bs=bs, bnk=bnk):
                        t.matmul(sp[bnk][:], ktile4[kq][bs:bs + 64, (kc % 8) * P:(kc % 8 + 1) * P],
                                 q_sb[h // 2][bs:bs + 64, :], start=True, stop=True)
                    ts_m = T_(_ts, waits=tw)

                    sw = [(pe, ts_m)]
                    if pv_m.get((h, kc - 4)) is not None:
                        sw.append((pe, pv_m[(h, kc - 4)]))
                    elif prev_pv.get(kc % 4) is not None:
                        sw.append((pe, prev_pv[kc % 4]))
                    if h == 0 and kc == 0 and qt == 0:
                        sw.append((vs, va))

                    def _se(s, h=h, kc=kc, bnk=bnk):
                        s.activation(pp_bf[kc % 4][:], sp[bnk][:], AF.Exp, bias=acol[:, h:h + 1])
                    exp_m[(h, kc)] = S(_se, waits=sw)
                    prev_exp[bnk] = exp_m[(h, kc)]

                    pw = [(ldv[vsl], vn_ld), (ac, exp_m[(h, kc)])]
                    if kc == 0 and att_evac.get(h - 2) is not None:
                        pw.append((vs, att_evac[h - 2]))
                    if kc == 0 and h < 2 and st["pb_free"][h % 2] is not None:
                        pw.append(st["pb_free"][h % 2])

                    def _tp(t, h=h, kc=kc, vsl=vsl):
                        t.matmul(pb[h % 2][:], vh_t[vsl][:].rearrange("p (c f) -> p c f", f=65)[:, kc, :],
                                 pp_bf[kc % 4][:], start=(kc == 0), stop=(kc == NKC - 1))
                    pv_m[(h, kc)] = T_(_tp, waits=pw)
                    prev_pv[kc % 4] = pv_m[(h, kc)]
                if h % 2 == 1:
                    for kq in range(NKT):
                        st["k_used"][kq] = pv_m[(h, NKC - 1)]
                st["v_used"][vsl] = pv_m[(h, NKC - 1)]

                def _vr(v, h=h):
                    v.reciprocal(rr_sb[:], pb[h % 2][64:65, :])
                vr = V(_vr, waits=[(pe, pv_m[(h, NKC - 1)])])

                def _tb2(t, h=h):
                    t.matmul(aux[0:64, :], ln["ones"][0:1, 0:64], rr_sb[0:1, :], start=True, stop=True)
                tb2 = T_(_tb2, waits=[(vs, vr), st["aux_free"]])

                def _sb2(s, h=h):
                    s.copy(rb_sb[:], aux[0:64, :])
                sb2 = S(_sb2, waits=[(pe, tb2)])
                st["aux_free"] = (ac, sb2)

                w = [(ac, sb2)]
                if h == 0 and st["att_used"] is not None:
                    w.append((pe, st["att_used"]))

                def _vat(v, h=h, bs=bs):
                    v.tensor_tensor(att_bf[h // 2][bs:bs + 64, :], pb[h % 2][0:64, :],
                                    rb_sb[:], ALU.mult)
                att_evac[h] = V(_vat, waits=w)
                st["pb_free"][h % 2] = (vs, att_evac[h])
            st["q_used"] = pv_m[(NH - 1, NKC - 1)]

            y3_ready = {}
            op_bank = {0: None, 1: None}
            prev_vo = None
            for m in range(8):
                obank = sp[0] if m % 2 == 0 else sp[2]

                ldy2[m % 2].n += 16
                y2n = ldy2[m % 2].n

                def _yy2(y, m=m, qt=qt, yu=st["y2r_used"][m % 2]):
                    if yu is not None:
                        y.wait_ge(vs.h, yu)
                    y.dma_start(out=y2res[m % 2][:],
                                in_=d["y2f_hbm"][m * P:(m + 1) * P, qt * 512:(qt + 1) * 512]
                                ).then_inc(ldy2[m % 2].h, 16)
                Y(_yy2, mile=False)

                w = [(vs, att_evac[NH - 1])]
                if op_bank[m % 2] is not None:
                    w.append((vs, op_bank[m % 2]))
                else:
                    w.append((ac, exp_m[(NH - 1, NKC - 1)]))

                wo_n = queue_bfload(weffoTr[:, :, m * P:(m + 1) * P], m % 2)
                w.append((ldw[m % 2], wo_n))

                def _to(t, m=m, obank=obank):
                    for i in range(8):
                        t.matmul(obank[:], wbf[m % 2][:, i * P:(i + 1) * P], att_bf[i][:],
                                 start=(i == 0), stop=(i == 7))
                to = T_(_to, waits=w)
                wuse[m % 2] = to

                w = [(pe, to)]

                w.append((ldy2[m % 2], y2n))
                if m == 0 and qh == 0 and st["acc_guard"] is not None:
                    w.append(st["acc_guard"])

                def _vo(v, m=m, qh=qh, obank=obank):
                    v.tensor_scalar(y3scr[:], obank[:], bo_c[:, m:m + 1], None, ALU.add)
                    v.tensor_tensor(y3f[m][:, qh * 512:(qh + 1) * 512], y3scr[:],
                                    y2res[m % 2][:], ALU.add)
                vo = V(_vo, waits=w)
                op_bank[m % 2] = vo
                y3_ready[m] = vo
                prev_vo = vo
                st["y2r_used"][m % 2] = vo
            st["att_used"] = pe.n

            cw = [[(vs, y3_ready[c])] for c in range(8)]
            bw = [[(pe, st["moe_mm_last"])] if st["moe_mm_last"] else [] for c in range(8)]
            lnd = emit_ln(ctx, V, S, T_,
                          [y3f[c][:, qh * 512:(qh + 1) * 512] for c in range(8)],
                          [y3b[c][:, qh * 512:(qh + 1) * 512] for c in range(8)],
                          ln, chunk_waits=cw, ybf_waits=bw)
            lnd_half[qh] = lnd
            st["sp_free"] = {0: (vs, lnd[-1]), 1: (vs, lnd[-1]), 2: (vs, lnd[-1])}
            st["aux_free"] = (vs, lnd[-1])

            def _tg(t, qh=qh):
                for i in range(8):
                    t.matmul(aux[0:8, :], gw_bf[i][:], y3b[i][:, qh * 512:(qh + 1) * 512],
                             start=(i == 0), stop=(i == 7))
            tg = T_(_tg, waits=[(vs, lnd[-1]), (gp, gwc[-1]), st["aux_free"]])

            def _sg(v):
                v.tensor_scalar(lg_sb[:], aux[0:8, :], gb_sb[0:8, 0:1], None, ALU.add)
            sg = V(_sg, waits=[(pe, tg)])
            st["aux_free"] = (vs, sg)

            last_vg = None
            for tc in range(4):
                tw1 = [(vs, sg)]
                if tc == 0:
                    tw1.append(st["aux_free"])
                if last_vg is not None:
                    tw1.append((vs, last_vg))

                def _tt1(t, tc=tc):
                    t.transpose(aux[:, 0:8], lg_sb[0:8, tc * P:(tc + 1) * P], id_sb[0:8, 0:8])
                tt1 = T_(_tt1, waits=tw1)

                def _vg1(v, tt1=tt1):
                    v.wait_ge(pe.h, tt1)
                    v.tensor_copy(lt_sb[:], aux[:, 0:8])
                    v.reduce_max(m1_sb[:], lt_sb[:], axis=AX.X)
                    v.tensor_scalar_mul(nm1_sb[:], m1_sb[:], -1.0)
                    v.tensor_scalar(msk_sb[:], lt_sb[:], m1_sb[:, 0:1], None, ALU.is_ge)
                    v.scalar_tensor_tensor(msk_sb[:], msk_sb[:], -1e30, lt_sb[:], ALU.mult, ALU.add)
                    v.reduce_max(m2_sb[:], msk_sb[:], axis=AX.X)
                vg1 = V(_vg1)

                def _sg1(s, vg1=vg1):
                    s.wait_ge(vs.h, vg1)
                    s.activation(rden_sb[:], m2_sb[:], AF.Exp, bias=nm1_sb[:, 0:1])
                    s.activation(g_tm[:], lt_sb[:], AF.Exp, bias=nm1_sb[:, 0:1])
                sg1 = S(_sg1)

                def _vg2(v, sg1=sg1, tc=tc, qh=qh):
                    v.wait_ge(ac.h, sg1)
                    v.tensor_scalar_add(rden_sb[:], rden_sb[:], 1.0)
                    v.reciprocal(rden_sb[:], rden_sb[:])
                    v.tensor_scalar(msk_sb[:], lt_sb[:], m2_sb[:, 0:1], None, ALU.is_ge)
                    v.tensor_tensor(g_tm[:], g_tm[:], msk_sb[:], ALU.mult)
                    v.tensor_scalar_mul(g_tm[:], g_tm[:], rden_sb[:, 0:1])
                vg2 = V(_vg2)

                tw2 = [(vs, vg2)]
                if tc == 0 and st["pb_free"][0] is not None:
                    tw2.append(st["pb_free"][0])

                def _tt2(t, tc=tc, qh=qh):
                    t.transpose(pb[0][0:8, tc * P:(tc + 1) * P], g_tm[:], id_sb[:])
                tt2 = T_(_tt2, waits=tw2)

                vw4 = [(pe, tt2)]
                if qh == 0 and tc == 0:
                    vw4.append((gfq, gfq.n))

                def _vg4(v, tc=tc, qh=qh):
                    v.tensor_copy(g_sb[0:8, qh * 512 + tc * P: qh * 512 + (tc + 1) * P],
                                  pb[0][0:8, tc * P:(tc + 1) * P])
                last_vg = V(_vg4, waits=vw4)
            g_done[qh] = last_vg
            st["pb_free"][0] = (vs, last_vg)

        w = [(vs, g_done[1])]
        if st["moe_mm_last"] is not None:
            w.append((pe, st["moe_mm_last"]))

        def _vg5(v):
            v.tensor_copy(g_bf[:], g_sb[:])
        vg5 = V(_vg5, waits=w)

        gfq.n += 32
        gfl_n = gfq.n

        def _gfl(g, mml=st["moe_mm_last"]):
            g.wait_ge(vs.h, vg5)
            if mml is not None:
                g.wait_ge(pe.h, mml)
            g.dma_start(out=gflat[0:1, :], in_=g_sb[0:4, :]).then_inc(gfq.h, 16)
            g.dma_start(out=gflat[64:65, :], in_=g_sb[4:8, :]).then_inc(gfq.h, 16)
        G(_gfl, mile=False)

        acc_init = {}
        prev_vb3 = None
        for qh2 in range(2):
            for dc in range(8):
                w = [(vs, vg5)]
                if qh2 == 0 and dc == 0:
                    w.append(st["sp_free"][2] or (vs, None))
                    if g0 is not None:
                        w.append((gp, g0))
                if prev_vb3 is not None:
                    w.append((vs, prev_vb3))

                def _tb3(t, qh2=qh2, dc=dc):
                    t.matmul(sp[2][:], b2_sb[0:8, dc * P:(dc + 1) * P],
                             g_bf[0:8, qh2 * 512:(qh2 + 1) * 512], start=True, stop=True)
                tb3 = T_(_tb3, waits=w)

                w = [(pe, tb3)]
                if st["acc_guard"] is not None:
                    w.append(st["acc_guard"])

                def _vb3(v, qh2=qh2, dc=dc):
                    v.tensor_tensor(y3f[dc][:, qh2 * 512:(qh2 + 1) * 512], sp[2][:],
                                    y3f[dc][:, qh2 * 512:(qh2 + 1) * 512], ALU.add)
                prev_vb3 = V(_vb3, waits=w)
                acc_init[(qh2, dc)] = prev_vb3
        st["sp_free"][2] = (vs, prev_vb3)

        acc_last = {}
        for ee in range(NE):
            h0_free = st["sp_free"][0]
            h1_free = st["sp_free"][1]
            act_done = {}
            for m in range(16):
                c0 = queue_wload(d["w0mTr"][ee][:, :, m * P:(m + 1) * P], 0)
                c1 = queue_wload(d["w1mTr"][ee][:, :, m * P:(m + 1) * P], 1)
                for qh2 in range(2):
                    w = [(gp, c0)]
                    if ee == 0 and m == 0 and qh2 == 0:
                        w.append((vs, lnd_half[1][-1]))
                    if h0_free is not None:
                        w.append(h0_free)

                    def _t0m(t, m=m, qh2=qh2):
                        for i in range(8):
                            t.matmul(sp[0][:], wbf[0][:, i * P:(i + 1) * P],
                                     y3b[i][:, qh2 * 512:(qh2 + 1) * 512],
                                     start=(i == 0), stop=(i == 7))
                    t0m = T_(_t0m, waits=w)
                    wuse[0] = t0m

                    w = [(gp, c1)]
                    if h1_free is not None:
                        w.append(h1_free)

                    def _t1m(t, m=m, qh2=qh2):
                        for i in range(8):
                            t.matmul(sp[1][:], wbf[1][:, i * P:(i + 1) * P],
                                     y3b[i][:, qh2 * 512:(qh2 + 1) * 512],
                                     start=(i == 0), stop=(i == 7))
                    t1m = T_(_t1m, waits=w)
                    wuse[1] = t1m

                    w = [(pe, t1m)]
                    if st["u1_used"][qh2] is not None:
                        w.append((vs, st["u1_used"][qh2]))

                    def _s1m(s, ee=ee, m=m, qh2=qh2):
                        s.activation(u1scr[qh2][:], sp[1][:], AF.Silu, bias=b1c_sb[ee][:, m:m + 1])
                    s1m = S(_s1m, waits=w)
                    h1_free = (ac, s1m)

                    def _v1m(v, ee=ee, m=m, qh2=qh2, s1m=s1m):
                        v.wait_ge(ac.h, s1m)
                        dst = act_e[m // 2][:, (m % 2) * 1024 + qh2 * 512:(m % 2) * 1024 + (qh2 + 1) * 512]
                        v.scalar_tensor_tensor(dst, sp[0][:], b0c_sb[ee][:, m:m + 1],
                                               u1scr[qh2][:], ALU.add, ALU.mult)
                    v1m = V(_v1m, waits=[(pe, t0m)])
                    h0_free = (vs, v1m)
                    st["u1_used"][qh2] = v1m
                    act_done[(m, qh2)] = v1m
            st["sp_free"][0] = h0_free
            st["sp_free"][1] = h1_free

            for qh2 in range(2):
                w = [(gfq, gfl_n), st["aux_free"]]

                def _tgb(t, ee=ee, qh2=qh2):
                    idx = ee * 1024 + qh2 * 512
                    if idx < 4096:
                        row, one = gflat[0:1, idx:idx + 512], ln["ones"][0:1, :]
                    else:
                        row, one = gflat[64:65, idx - 4096:idx - 4096 + 512], ln["ones"][64:65, :]
                    t.matmul(aux[:], one, row, start=True, stop=True)
                tgb = T_(_tgb, waits=w)

                w = [(pe, tgb)]
                if st["gbv_used"] is not None:
                    w.append((vs, st["gbv_used"]))

                def _vgb(v):
                    v.tensor_copy(gbv_sb[:], aux[:])
                vgb = V(_vgb, waits=w)
                st["aux_free"] = (vs, vgb)

                for dc in range(8):
                    half = dc % 2
                    c2 = queue_wload(d["w2mTr"][ee][:, 0:8, dc * P:(dc + 1) * P], 0)
                    c3 = queue_wload(d["w2mTr"][ee][:, 8:16, dc * P:(dc + 1) * P], 1)
                    w = [(gp, c2), (ee == 0 and qh2 == 0 and dc == 0) and st["sp_free"][2] or (vs, None)]
                    w = [(gp, c2)]
                    if st["sp_free"][2] is not None:
                        w.append(st["sp_free"][2])

                    def _t2a(t, qh2=qh2):
                        for i in range(8):
                            t.matmul(sp[2][:], wbf[0][:, i * P:(i + 1) * P],
                                     act_e[i // 2][:, (i % 2) * 1024 + qh2 * 512:(i % 2) * 1024 + (qh2 + 1) * 512],
                                     start=(i == 0), stop=False)
                    t2a = T_(_t2a, waits=w + [(vs, act_done[(15, qh2)])])
                    wuse[0] = t2a

                    def _t2b(t, qh2=qh2):
                        for i in range(8, 16):
                            t.matmul(sp[2][:], wbf[1][:, (i - 8) * P:(i - 7) * P],
                                     act_e[i // 2][:, (i % 2) * 1024 + qh2 * 512:(i % 2) * 1024 + (qh2 + 1) * 512],
                                     start=False, stop=(i == 15))
                    t2b = T_(_t2b, waits=[(gp, c3)])
                    wuse[1] = t2b

                    w = [(pe, t2b), (vs, vgb)]
                    if st["tmp_used"] is not None:
                        w.append((vs, st["tmp_used"]))

                    def _v2a(v):
                        v.tensor_tensor(tmp_f[:], sp[2][:], gbv_sb[:], ALU.mult)
                    v2a = V(_v2a, waits=w)
                    st["sp_free"][2] = (vs, v2a)

                    wprev = acc_init[(qh2, dc)] if ee == 0 else acc_last[(qh2, dc)]

                    def _v2b(v, qh2=qh2, dc=dc):
                        v.tensor_tensor(y3f[dc][:, qh2 * 512:(qh2 + 1) * 512],
                                        y3f[dc][:, qh2 * 512:(qh2 + 1) * 512], tmp_f[:], ALU.add)
                    v2b = V(_v2b, waits=[(vs, wprev)])
                    acc_last[(qh2, dc)] = v2b
                    st["tmp_used"] = v2b
                st["gbv_used"] = v2b
        st["moe_mm_last"] = pe.n

        store_n0 = stout.n
        for qh2 in range(2):
            cw = [[(vs, acc_last[(qh2, c)])] for c in range(8)]
            lnf = emit_ln(ctx, V, S, T_,
                          [y3f[c][:, qh2 * 512:(qh2 + 1) * 512] for c in range(8)],
                          None, ln, chunk_waits=cw)
            st["sp_free"][2] = (vs, lnf[-1])
            st["aux_free"] = (vs, lnf[-1])
            for c in range(8):
                stout.n += 16

                def _gso(g, c=c, qh2=qh2, qo=qo, lnf=lnf):
                    g.wait_ge(vs.h, lnf[c])
                    g.dma_start(out=d["outT"][c * P:(c + 1) * P,
                                             qo * 1024 + qh2 * 512: qo * 1024 + (qh2 + 1) * 512],
                                in_=y3f[c][:, qh2 * 512:(qh2 + 1) * 512]).then_inc(stout.h, 16)
                G(_gso, mile=False)
        st["acc_guard"] = (stout, stout.n)

    def _gfin(g, nf=stout.n):
        g.wait_ge(stout.h, nf)
    G(_gfin, mile=False)
    ctx.flush()



def _col_layout(v, rows=P):
    n = v.shape[0]
    return np.ascontiguousarray(v.reshape(n // rows, rows).T)


def prep_inputs(inputs):
    t = {}
    t["ident"] = np.eye(P, dtype=np.float32)
    t["wqkvT"] = np.ascontiguousarray(inputs["L0_attn_W_qkv"].T)
    t["w1T"] = np.ascontiguousarray(inputs["L0_ffn_w1_w"].T)
    t["w2T"] = np.ascontiguousarray(inputs["L0_ffn_w2_w"].T)
    t["h1T"] = np.ascontiguousarray(inputs["L0_ffn_h1_w"].T)
    t["h2T"] = np.ascontiguousarray(inputs["L0_ffn_h2_w"].T)
    t["w3T"] = np.ascontiguousarray(inputs["L0_ffn_w3_w"].T)
    t["s1c"] = _col_layout(inputs["L0_ffn_w1_s"])
    t["s2c"] = _col_layout(inputs["L0_ffn_w2_s"])
    t["sh1c"] = _col_layout(inputs["L0_ffn_h1_s"])
    t["sh2c"] = _col_layout(inputs["L0_ffn_h2_s"])
    t["s3c"] = _col_layout(inputs["L0_ffn_w3_s"])
    for p, Pn in [("q", "q"), ("k", "k"), ("v", "v"), ("o", "o")]:
        Pu = Pn.capitalize() if False else Pn
        t["w_" + p] = np.ascontiguousarray(inputs[f"L1_attn_W{Pn}"].T)
        t["a_" + p] = np.ascontiguousarray(inputs[f"L1_attn_A{Pn}"])
        t["bt_" + p] = np.ascontiguousarray(inputs[f"L1_attn_B{Pn}"].T)
    t["bq_c"] = _col_layout(inputs["L1_attn_bq"])
    t["bk_c"] = _col_layout(inputs["L1_attn_bk"])
    t["bv_row"] = np.ascontiguousarray(inputs["L1_attn_bv"].reshape(1, D))
    t["bo_c"] = _col_layout(inputs["L1_attn_bo"])
    t["gwT"] = np.ascontiguousarray(inputs["L1_ffn_gate_w"].T)
    t["gb_c"] = np.ascontiguousarray(inputs["L1_ffn_gate_b"].reshape(NE, 1))
    t["w0mT"] = np.ascontiguousarray(inputs["L1_ffn_w0"].transpose(0, 2, 1))
    t["w1mT"] = np.ascontiguousarray(inputs["L1_ffn_w1"].transpose(0, 2, 1))
    t["w2mT"] = np.ascontiguousarray(inputs["L1_ffn_w2"].transpose(0, 2, 1))
    t["b0c"] = np.ascontiguousarray(inputs["L1_ffn_b0"].reshape(NE, 16, P).transpose(0, 2, 1))
    t["b1c"] = np.ascontiguousarray(inputs["L1_ffn_b1"].reshape(NE, 16, P).transpose(0, 2, 1))
    t["b2m"] = np.ascontiguousarray(inputs["L1_ffn_b2"])
    return t


def declare_stage1(nc, T, kinds=("ExternalInput", "ExternalOutput")):
    ki, ko = kinds
    d = {}
    d["xT"] = nc.dram_tensor("xT", [D, T], F32, kind=ki).ap()
    d["wqkvT"] = nc.dram_tensor("wqkvT", [D, 3072], F32, kind=ki).ap()
    d["ident"] = nc.dram_tensor("ident", [P, P], F32, kind=ki).ap()
    d["qf_hbm"] = nc.dram_tensor("qf_hbm", [T, D], BF16, kind=ko).ap()
    d["kv2_hbm"] = nc.dram_tensor("kv2_hbm", [P, 2048], F32, kind=ko).ap()
    d["ksrow_hbm"] = nc.dram_tensor("ksrow_hbm", [1, D], F32, kind=ko).ap()
    return d


def declare_stage2(nc, T, d=None, kinds=("ExternalInput", "ExternalOutput")):
    ki, ko = kinds
    if d is None:
        d = {}
    def di(name, shape, dt, kind):
        if name not in d:
            d[name] = nc.dram_tensor(name, shape, dt, kind=kind).ap()
    di("xT", [D, T], F32, ki)
    di("qf_hbm", [T, D], BF16, ki)
    di("kv2_hbm", [P, 2048], F32, ki)
    di("ksrow_hbm", [1, D], F32, ki)
    di("s1c", [P, 32], F32, ki)
    di("s2c", [P, 32], F32, ki)
    di("sh1c", [P, 32], F32, ki)
    di("sh2c", [P, 32], F32, ki)
    di("s3c", [P, 8], F32, ki)
    for nm, shp in [("w1T", [D, DH]), ("w2T", [D, DH]), ("h1T", [DH, DH]),
                    ("h2T", [DH, DH]), ("w3T", [DH, D])]:
        if nm not in d:
            d[nm] = nc.dram_tensor(nm, shp, F32, kind=ki).ap()
        d[nm + "r"] = d[nm].rearrange("(c p) m -> p c m", p=P)
    di("qfs_hbm", [T, D], BF16, "Internal")
    di("y2f_hbm", [D, T], F32, ko)
    di("y2b_hbm", [D, T], BF16, ko)
    return d


def declare_stage3(nc, T, d=None, kinds=("ExternalInput", "ExternalOutput")):
    ki, ko = kinds
    if d is None:
        d = {}

    def di(name, shape, dt, kind):
        if name not in d:
            d[name] = nc.dram_tensor(name, shape, dt, kind=kind).ap()
    di("y2b_hbm", [D, T], BF16, ki)
    di("ident", [P, P], F32, ki)
    for p in ["q", "k", "v", "o"]:
        di("w_" + p, [D, D], F32, ki)
        di("a_" + p, [8, D], F32, ki)
        di("bt_" + p, [8, D], F32, ki)
    di("bq_c", [P, 8], F32, ki)
    di("bk_c", [P, 8], F32, ki)
    di("bv_row", [1, D], F32, ki)
    di("q_hbm", [D, T], BF16, ko)
    di("k_hbm", [D, T], BF16, ko)
    di("vhat_hbm", [NH, T, 65], BF16, ko)
    di("weffoT_hbm", [D, D], BF16, ko)
    di("negarow_hbm", [1, NH], F32, ko)
    return d


def declare_stage4(nc, T, d=None, kinds=("ExternalInput", "ExternalOutput")):
    ki, ko = kinds
    if d is None:
        d = {}

    def di(name, shape, dt, kind):
        if name not in d:
            d[name] = nc.dram_tensor(name, shape, dt, kind=kind).ap()
    di("q_hbm", [D, T], BF16, ki)
    di("k_hbm", [D, T], BF16, ki)
    di("vhat_hbm", [NH, T, 65], BF16, ki)
    di("weffoT_hbm", [D, D], BF16, ki)
    di("negarow_hbm", [1, NH], F32, ki)
    di("y2f_hbm", [D, T], F32, ki)
    di("ident", [P, P], F32, ki)
    di("bo_c", [P, 8], F32, ki)
    di("gwT", [D, 8], F32, ki)
    di("gb_c", [8, 1], F32, ki)
    di("w0mT", [NE, D, MH], F32, ki)
    di("w1mT", [NE, D, MH], F32, ki)
    di("w2mT", [NE, MH, D], F32, ki)
    di("b0c", [NE, P, 16], F32, ki)
    di("b1c", [NE, P, 16], F32, ki)
    di("b2m", [NE, D], F32, ki)
    di("outT", [D, T], F32, ko)
    for nm in ["w0mT", "w1mT", "w2mT"]:
        d[nm + "r"] = [d[nm][ee].rearrange("(c p) m -> p c m", p=P) for ee in range(NE)]
    return d


S1_KEYS = ["xT", "wqkvT", "ident"]
S2_KEYS = ["xT", "qf_hbm", "kv2_hbm", "ksrow_hbm", "s1c", "s2c", "sh1c", "sh2c", "s3c",
           "w1T", "w2T", "h1T", "h2T", "w3T"]
S3_KEYS = ["y2b_hbm", "ident", "w_q", "a_q", "bt_q", "w_k", "a_k", "bt_k",
           "w_v", "a_v", "bt_v", "w_o", "a_o", "bt_o", "bq_c", "bk_c", "bv_row"]
S4_KEYS = ["q_hbm", "k_hbm", "vhat_hbm", "weffoT_hbm", "negarow_hbm", "y2f_hbm",
           "ident", "bo_c", "gwT", "gb_c", "w0mT", "w1mT", "w2mT", "b0c", "b1c", "b2m"]


def _build(stage, T):
    nc = bass.Bass("TRN2", target_bir_lowering=False)
    ex = ExitStack()
    declare, emit = {
        1: (declare_stage1, emit_stage1),
        2: (declare_stage2, emit_stage2),
        3: (declare_stage3, emit_stage3),
        4: (declare_stage4, emit_stage4),
    }[stage]
    d = declare(nc, T)
    with ExitStack() as ex:
        ctx = Ctx(nc, ex, f"s{stage}")
        emit(ctx, d, T)
    return nc


def build_fused(T):
    nc = bass.Bass("TRN2", target_bir_lowering=False)
    d = {}
    declare_stage1(nc, T, kinds=("ExternalInput", "Internal"))
    return None


def _declare_all(nc, T):
    d = {}

    def di(name, shape, dt, kind):
        if name not in d:
            d[name] = nc.dram_tensor(name, shape, dt, kind=kind).ap()
    di("xT", [D, T], F32, "ExternalInput")
    di("ident", [P, P], F32, "ExternalInput")
    di("wqkvT", [D, 3072], F32, "ExternalInput")
    for nm, shape, dt in [("qf_hbm", [T, D], BF16), ("kv2_hbm", [P, 2048], F32),
                          ("ksrow_hbm", [1, D], F32), ("qfs_hbm", [T, D], BF16),
                          ("y2f_hbm", [D, T], F32), ("y2b_hbm", [D, T], BF16),
                          ("q_hbm", [D, T], BF16), ("k_hbm", [D, T], BF16),
                          ("vhat_hbm", [NH, T, 65], BF16), ("weffoT_hbm", [D, D], BF16),
                          ("negarow_hbm", [1, NH], F32)]:
        di(nm, shape, dt, "Internal")
    declare_stage2(nc, T, d=d)
    declare_stage3(nc, T, d=d)
    declare_stage4(nc, T, d=d)
    d["outT"] = d.get("outT") or nc.dram_tensor("outT", [D, T], F32, kind="ExternalOutput").ap()
    return d


def build_fused_nc(T):
    nc = bass.Bass("TRN2", target_bir_lowering=False)
    master = ExitStack()
    with master:
        d = _declare_all(nc, T)
        done = Sem(master.enter_context(nc.semaphore("stage_done")))
        for stage, emit in [(1, emit_stage1), (2, emit_stage2),
                            (3, emit_stage3), (4, emit_stage4)]:
            with ExitStack() as ex:
                ctx = Ctx(nc, ex, f"s{stage}", sem_ex=master)
                if stage > 1:
                    ctx.barrier_in(done, done.n)
                emit(ctx, d, T)
                ctx.barrier_out(done)
    return nc


ALL_IN_KEYS = ["xT", "ident", "wqkvT", "s1c", "s2c", "sh1c", "sh2c", "s3c",
               "w1T", "w2T", "h1T", "h2T", "w3T",
               "w_q", "a_q", "bt_q", "w_k", "a_k", "bt_k",
               "w_v", "a_v", "bt_v", "w_o", "a_o", "bt_o",
               "bq_c", "bk_c", "bv_row", "bo_c", "gwT", "gb_c",
               "w0mT", "w1mT", "w2mT", "b0c", "b1c", "b2m"]


def kernel(**inputs):
    x = np.asarray(inputs["L0_attn_x"])
    B, T, _ = x.shape
    sh = prep_inputs(inputs)
    nc = build_fused_nc(T)
    maps = []
    for b in range(B):
        m = {k: sh[k] for k in ALL_IN_KEYS if k != "xT"}
        m["xT"] = np.ascontiguousarray(x[b].T)
        maps.append(m)
    res = run_bass_kernel_spmd(nc, maps, core_ids=list(range(B))).results
    out = np.stack([np.asarray(res[b]["outT"]).T for b in range(B)], axis=0)
    return np.ascontiguousarray(out.astype(np.float32))


def kernel_multilaunch(**inputs):
    x = np.asarray(inputs["L0_attn_x"])
    B, T, _ = x.shape
    sh = prep_inputs(inputs)
    xT = [np.ascontiguousarray(x[b].T) for b in range(B)]
    cores = list(range(B))

    def run(stage, per_core, keys):
        nc = _build(stage, T)
        maps = []
        for b in range(B):
            m = {}
            for k in keys:
                m[k] = per_core[b][k] if k in per_core[b] else sh[k]
            maps.append(m)
        return run_bass_kernel_spmd(nc, maps, core_ids=cores).results

    pc = [{"xT": xT[b]} for b in range(B)]
    r1 = run(1, pc, S1_KEYS)
    for b in range(B):
        pc[b].update({k: r1[b][k] for k in ["qf_hbm", "kv2_hbm", "ksrow_hbm"]})
    r2 = run(2, pc, S2_KEYS)
    for b in range(B):
        pc[b].update({k: r2[b][k] for k in ["y2f_hbm", "y2b_hbm"]})
    r3 = run(3, pc, S3_KEYS)
    for b in range(B):
        pc[b].update({k: r3[b][k] for k in
                      ["q_hbm", "k_hbm", "vhat_hbm", "weffoT_hbm", "negarow_hbm"]})
    r4 = run(4, pc, S4_KEYS)
    out = np.stack([np.asarray(r4[b]["outT"]).T for b in range(B)], axis=0)
    return np.ascontiguousarray(out.astype(np.float32))
